# version 06
# baseline (speedup 1.0000x reference)
import numpy as np
import concourse.bass as bass
import concourse.mybir as mybir
from concourse.bass_utils import run_bass_kernel_spmd
from contextlib import ExitStack

F32 = mybir.dt.float32
BF16 = mybir.dt.bfloat16
AF = mybir.ActivationFunctionType
ALU = mybir.AluOpType
AX = mybir.AxisListType

P = 128
D = 2048
H = 8192
KD = D // P
HS = H // P
NCORES = 8

SEM_NAMES = ["tt", "vv", "ss", "cc",
             "xc", "xl0", "xl1", "gw0", "gw1", "wu0", "wu1", "ca", "gi",
             "xp0", "xp1", "wd0", "wd1",
             "sp0", "sp1", "sp2", "sp3", "sp4", "sp5", "sp6", "sp7",
             "sc", "sg0", "sg1", "sg2", "sg3", "so0", "so1",
             "wsc", "dv"]


class Cnt:
    def __init__(self, names):
        self.v = {n: 0 for n in names}

    def inc(self, n, k=1):
        self.v[n] += k
        return self.v[n]

    def __getitem__(self, n):
        return self.v[n]


def build_nc(tcore, tt, t_total):
    ntb = tcore // tt
    assert tcore % tt == 0 and tt <= 512
    assert ntb <= 4
    ltt = min(512, tt)
    lnt = tcore // ltt

    nc = bass.Bass("TRN2", target_bir_lowering=False, debug=False)

    xT = nc.dram_tensor("xT", [D, tcore], F32, kind="ExternalInput").ap()
    gw = nc.dram_tensor("gw", [KD, KD, P, P], F32, kind="ExternalInput").ap()
    wg = nc.dram_tensor("wg", [HS, KD, P, P], F32, kind="ExternalInput").ap()
    wu = nc.dram_tensor("wu", [HS, KD, P, P], F32, kind="ExternalInput").ap()
    wdr = nc.dram_tensor("wdr", [KD, HS, P, P], F32, kind="ExternalInput").ap()
    avec = nc.dram_tensor("avec", [P, KD], F32, kind="ExternalInput").ap()
    bvec = nc.dram_tensor("bvec", [P, KD], F32, kind="ExternalInput").ap()
    cvec = nc.dram_tensor("cvec", [P, KD], F32, kind="ExternalInput").ap()
    dvec = nc.dram_tensor("dvec", [P, KD], F32, kind="ExternalInput").ap()
    gbvec = nc.dram_tensor("gbvec", [P, KD], F32, kind="ExternalInput").ap()
    ulr = nc.dram_tensor("ulr", [KD, P, 4], F32, kind="ExternalInput").ap()
    vlr = nc.dram_tensor("vlr", [4, D], F32, kind="ExternalInput").ap()
    maskin = nc.dram_tensor("maskin", [P, KD * NCORES], F32, kind="ExternalInput").ap()
    outT = nc.dram_tensor("outT", [D, tcore], F32, kind="ExternalOutput").ap()

    gated_sc = nc.dram_tensor("gated_sc", [H, tcore], BF16).ap()
    wd_bf = nc.dram_tensor("wd_bf", [KD, P, HS * P], BF16).ap()
    pc_sc = nc.dram_tensor("pc_sc", [D, tcore], BF16).ap()
    cc_in = nc.dram_tensor("cc_in", [P, 2 * KD], F32).ap()
    cc_out = nc.dram_tensor("cc_out", [NCORES * P, 2 * KD], F32).ap()

    C = Cnt(SEM_NAMES)

    with ExitStack() as ctx:
        S = {n: ctx.enter_context(nc.semaphore(n)) for n in SEM_NAMES}

        sb = lambda name, shape, dt: ctx.enter_context(nc.sbuf_tensor(name, shape, dt))
        dvn = [0]

        def vmark(inst):
            inst.then_inc(S["dv"], 1)
            dvn[0] += 1
            return dvn[0]

        def vwait(vector):
            vector.wait_ge(S["dv"], dvn[0])
        a_c = sb("a_c", [P, KD], F32)
        b_c = sb("b_c", [P, KD], F32)
        c_c = sb("c_c", [P, KD], F32)
        d_c = sb("d_c", [P, KD], F32)
        gb_c = sb("gb_c", [P, KD], F32)
        aa_c = sb("aa_c", [P, KD], F32)
        cb_c = sb("cb_c", [P, KD], F32)
        qm1 = sb("qm1", [P, KD], F32)
        dlr_c = sb("dlr_c", [P, KD], F32)
        deff_c = sb("deff_c", [P, KD], F32)
        carry = sb("carry", [P, KD], F32)
        hcar = sb("hcar", [P, KD], F32)
        Cc0 = sb("Cc0", [P, KD], F32)
        Cc1 = sb("Cc1", [P, KD], F32)
        usum_sl = sb("usum_sl", [P, KD * ntb], F32)
        pack_sb = sb("pack_sb", [P, 2 * KD], F32)
        mask_rep = sb("mask_rep", [P, KD * NCORES], F32)
        qeff_all = sb("qeff_all", [P, KD * NCORES], F32)
        mlt_all = sb("mlt_all", [P, KD * NCORES], F32)
        U_sb = sb("U_sb", [P, KD * 4], F32)
        V_sb = sb("V_sb", [4, D], F32)
        P4_sb = sb("P4_sb", [4, 1], F32)
        cc_all = sb("cc_all", [P, NCORES * 2 * KD], F32)
        ones_col = sb("ones_col", [P, 1], F32)
        ones_row = sb("ones_row", [1, P], F32)
        zeros_tt = sb("zeros_tt", [P, 512], F32)

        s_buf = sb("s_buf", [P, tt], F32)
        t2_buf = sb("t2_buf", [P, tt], F32)
        pcst = [sb("pcst%d" % i, [P, tt], BF16) for i in range(8)]

        es_xn = ExitStack()
        xn_bf = es_xn.enter_context(
            nc.sbuf_tensor("xn_bf", [P, KD * tcore], BF16, side="right"))
        ug_holder = {}

        banks = [ctx.enter_context(nc.psum_tensor("pb%d" % i, [P, 512], F32))
                 for i in range(8)]

        block = ctx.enter_context(nc.Block())

        def xn_ap(c, t0, n):
            return xn_bf[:, c * tcore + t0: c * tcore + t0 + n]

        def ug_ap(c, t0, n):
            ug_bf = ug_holder["t"]
            return ug_bf[:, c * tcore + t0: c * tcore + t0 + n]

        _sid_setup = nc.enter_named_scope("ph_setup", False)[0]
        @block.sync
        def _(sync):
            for src, dst in [(avec, a_c), (bvec, b_c), (cvec, c_c), (dvec, d_c),
                             (gbvec, gb_c), (maskin, mask_rep), (vlr, V_sb)]:
                sync.dma_start(out=dst[:], in_=src[:]).then_inc(S["xc"], 16)
                C.inc("xc", 16)
            sync.dma_start(out=U_sb[:].rearrange("p (c r) -> p c r", c=KD),
                           in_=ulr[:].transpose([1, 0, 2])).then_inc(S["xc"], 16)
            C.inc("xc", 16)
        m_consts = C["xc"]

        @block.scalar
        def _(scalar):
            scalar.wait_ge(S["xc"], m_consts)
            scalar.activation(out=aa_c[:], in_=a_c[:],
                              func=AF.Tanh).then_inc(S["ss"], 1)
            C.inc("ss")
        m_aa = C["ss"]

        @block.vector
        def _(vector):
            vector.memset(ones_col[:], 1.0)
            vector.memset(ones_row[:], 1.0)
            vector.memset(zeros_tt[:], 0.0)
            vector.memset(carry[:], 0.0)
            vector.memset(usum_sl[:], 0.0)
            vector.wait_ge(S["xc"], m_consts)
            vector.wait_ge(S["ss"], m_aa)
            vector.tensor_mul(out=cb_c[:], in0=c_c[:], in1=b_c[:])
            vmark(vector.tensor_mul(out=qm1[:], in0=aa_c[:], in1=aa_c[:]))
            nsq = int(round(np.log2(tcore)))
            assert 2 ** nsq == tcore
            for _i in range(nsq - 1):
                vwait(vector)
                vmark(vector.tensor_mul(out=qm1[:], in0=qm1[:], in1=qm1[:]))
            vwait(vector)
            vmark(vector.tensor_scalar_add(out=qm1[:], in0=qm1[:], scalar1=-1.0))
            vwait(vector)
            vector.engine_nop().then_inc(S["vv"], 1)
            C.inc("vv")
        m_setup_v = C["vv"]

        nc.leave_named_scope("ph_setup", _sid_setup, False)
        _sid_L = nc.enter_named_scope("ph_L", False)[0]
        es_L = ExitStack()
        x_sbs = [es_L.enter_context(
            nc.sbuf_tensor("x_sb%d" % i, [P, KD * ltt], F32, side="right"))
            for i in range(2)]
        sq_sb = es_L.enter_context(
            nc.sbuf_tensor("sq_sb", [P, KD * ltt], F32, side="right"))
        t1x = [es_L.enter_context(
            nc.sbuf_tensor("t1x%d" % i, [P, ltt], F32, side="right"))
            for i in range(2)]
        srows = es_L.enter_context(
            nc.sbuf_tensor("srows", [1, 7 * ltt], F32, side="right"))
        mrow = srows[:, 0 * ltt:1 * ltt]
        e2row = srows[:, 1 * ltt:2 * ltt]
        vrow = srows[:, 2 * ltt:3 * ltt]
        r0row = srows[:, 3 * ltt:4 * ltt]
        t1row = srows[:, 4 * ltt:5 * ltt]
        rstdrow = srows[:, 5 * ltt:6 * ltt]
        nmrow = srows[:, 6 * ltt:7 * ltt]

        xn_done = [None] * lnt
        stats_done = [None] * lnt
        m_invs = [None] * lnt

        for k in range(lnt):
            x_sb = x_sbs[k % 2]
            xls = "xl%d" % (k % 2)

            @block.sync
            def _(sync):
                if k >= 2:
                    sync.wait_ge(S["vv"], xn_done[k - 2])
                for c in range(KD):
                    sync.dma_start(
                        out=x_sb[:, c * ltt:(c + 1) * ltt],
                        in_=xT[c * P:(c + 1) * P, k * ltt:(k + 1) * ltt],
                    ).then_inc(S[xls], 16)
                    C.inc(xls, 16)
            m_xl = C[xls]

            ss_base = C["ss"]

            @block.scalar
            def _(scalar):
                if k >= 1:
                    scalar.wait_ge(S["tt"], stats_done[k - 1])
                scalar.wait_ge(S[xls], m_xl)
                for c in range(KD):
                    scalar.activation(
                        out=sq_sb[:, c * ltt:(c + 1) * ltt],
                        in_=x_sb[:, c * ltt:(c + 1) * ltt],
                        func=AF.Square).then_inc(S["ss"], 1)
                    C.inc("ss")

            @block.tensor
            def _(tensor):
                if k >= 1:
                    tensor.wait_ge(S["vv"], m_invs[k - 1])
                else:
                    tensor.wait_ge(S["vv"], m_setup_v)
                mm = None
                for c in range(KD):
                    tensor.wait_ge(S["ss"], ss_base + c + 1)
                    tensor.matmul(banks[0][0:1, 0:ltt], ones_col[:],
                                  x_sb[:, c * ltt:(c + 1) * ltt],
                                  start=(c == 0), stop=(c == KD - 1))
                    mm = tensor.matmul(banks[1][0:1, 0:ltt], ones_col[:],
                                       sq_sb[:, c * ltt:(c + 1) * ltt],
                                       start=(c == 0), stop=(c == KD - 1))
                mm.then_inc(S["tt"], 1)
                stats_done[k] = C.inc("tt")

            @block.vector
            def _(vector):
                vector.wait_ge(S["tt"], stats_done[k])
                inv = 1.0 / D
                vector.tensor_scalar_mul(out=mrow, in0=banks[0][0:1, 0:ltt],
                                         scalar1=inv)
                vmark(vector.tensor_scalar_mul(out=e2row, in0=banks[1][0:1, 0:ltt],
                                               scalar1=inv))
                vwait(vector)
                vmark(vector.tensor_mul(out=t1row, in0=mrow, in1=mrow))
                vwait(vector)
                vmark(vector.tensor_sub(out=vrow, in0=e2row, in1=t1row))
                vwait(vector)
                vmark(vector.tensor_scalar_add(out=vrow, in0=vrow, scalar1=1e-5))
                vwait(vector)
                vector.reciprocal(out=r0row, in_=vrow).then_inc(S["vv"], 1)
                m_invs[k] = C.inc("vv")
            m_inv = C["vv"]

            @block.scalar
            def _(scalar):
                scalar.wait_ge(S["vv"], m_inv)
                scalar.activation(out=r0row, in_=r0row,
                                  func=AF.Sqrt).then_inc(S["ss"], 1)
                C.inc("ss")
            m_sqrt = C["ss"]

            @block.vector
            def _(vector):
                vector.wait_ge(S["ss"], m_sqrt)
                vmark(vector.tensor_mul(out=t1row, in0=r0row, in1=r0row))
                vwait(vector)
                vmark(vector.tensor_mul(out=t1row, in0=t1row, in1=vrow))
                vwait(vector)
                vmark(vector.tensor_scalar(out=t1row, in0=t1row, scalar1=-0.5,
                                           scalar2=1.5, op0=ALU.mult, op1=ALU.add))
                vwait(vector)
                vmark(vector.tensor_mul(out=rstdrow, in0=r0row, in1=t1row))
                vwait(vector)
                vmark(vector.tensor_mul(out=nmrow, in0=mrow, in1=rstdrow))
                vwait(vector)
                vector.tensor_scalar_mul(out=nmrow, in0=nmrow,
                                         scalar1=-1.0).then_inc(S["vv"], 1)
                C.inc("vv")
            m_rows = C["vv"]

            @block.tensor
            def _(tensor):
                tensor.wait_ge(S["vv"], m_rows)
                tensor.matmul(banks[2][:, 0:ltt], ones_row[:], rstdrow,
                              start=True, stop=True)
                tensor.matmul(banks[3][:, 0:ltt], ones_row[:], nmrow,
                              start=True, stop=True).then_inc(S["tt"], 1)
                C.inc("tt")
            m_bc = C["tt"]

            @block.vector
            def _(vector):
                vector.wait_ge(S["tt"], m_bc)
                last = None
                for c in range(KD):
                    if c >= 2:
                        vwait(vector)
                    vmark(vector.tensor_mul(out=t1x[c % 2][:],
                                            in0=x_sb[:, c * ltt:(c + 1) * ltt],
                                            in1=banks[2][:, 0:ltt]))
                    vwait(vector)
                    last = vector.tensor_add(out=xn_ap(c, k * ltt, ltt),
                                             in0=t1x[c % 2][:],
                                             in1=banks[3][:, 0:ltt])
                    if c < KD - 1:
                        vmark(last)
                last.then_inc(S["vv"], 1)
                xn_done[k] = C.inc("vv")

        nc.leave_named_scope("ph_L", _sid_L, False)
        _sid_G = nc.enter_named_scope("ph_G", False)[0]
        es_L.close()
        ug_holder["t"] = ctx.enter_context(
            nc.sbuf_tensor("ug_bf", [P, KD * tcore], BF16))
        es_G = ExitStack()
        gw_pan = [es_G.enter_context(nc.sbuf_tensor("gwp%d" % i, [P, KD * P], BF16))
                  for i in range(2)]
        gs_bufs = [es_G.enter_context(nc.sbuf_tensor("gs%d" % i, [P, tt], F32))
                   for i in range(ntb)]

        tt_g_base = C["tt"]
        ss_g_base = C["ss"]
        vv_g_base = C["vv"]
        gwsem = ["gw0", "gw1"]

        @block.gpsimd
        def _(gpsimd):
            gpsimd.wait_ge(S["vv"], xn_done[lnt - 1])
            for es in range(KD):
                if es >= 2:
                    gpsimd.wait_ge(S["tt"], tt_g_base + es - 1)
                gpsimd.dma_start(
                    out=gw_pan[es % 2][:].rearrange("p (c j) -> p c j", c=KD),
                    in_=gw[es].transpose([1, 0, 2])).then_inc(S[gwsem[es % 2]], 16)
                C.inc(gwsem[es % 2], 16)

        @block.tensor
        def _(tensor):
            tensor.wait_ge(S["vv"], xn_done[lnt - 1])
            for es in range(KD):
                tensor.wait_ge(S[gwsem[es % 2]], 16 * (es // 2 + 1))
                if es >= 2:
                    tensor.wait_ge(S["ss"], ss_g_base + (es - 1) * ntb)
                grp = (es % 2) * 4
                mm = None
                for kd in range(KD):
                    for tb in range(ntb):
                        mm = tensor.matmul(
                            banks[grp + tb][:, 0:tt],
                            gw_pan[es % 2][:, kd * P:(kd + 1) * P],
                            xn_ap(kd, tb * tt, tt),
                            start=(kd == 0), stop=(kd == KD - 1))
                mm.then_inc(S["tt"], 1)
                C.inc("tt")

        @block.scalar
        def _(scalar):
            for es in range(KD):
                scalar.wait_ge(S["tt"], tt_g_base + es + 1)
                if es >= 1:
                    scalar.wait_ge(S["vv"], vv_g_base + es)
                grp = (es % 2) * 4
                for tb in range(ntb):
                    scalar.activation(
                        out=gs_bufs[tb][:], in_=banks[grp + tb][:, 0:tt],
                        func=AF.Sigmoid, bias=gb_c[:, es:es + 1],
                    ).then_inc(S["ss"], 1)
                    C.inc("ss")

        @block.vector
        def _(vector):
            for es in range(KD):
                for tb in range(ntb):
                    vector.wait_ge(S["ss"], ss_g_base + es * ntb + tb + 1)
                    inst = vector.scalar_tensor_tensor(
                        out=ug_ap(es, tb * tt, tt),
                        in0=gs_bufs[tb][:], scalar=1.0,
                        in1=xn_ap(es, tb * tt, tt),
                        op0=ALU.mult, op1=ALU.mult,
                        accum_out=usum_sl[:, es * ntb + tb:es * ntb + tb + 1])
                inst.then_inc(S["vv"], 1)
                C.inc("vv")
        vv_g_end = C["vv"]
        ss_g_end = C["ss"]

        nc.leave_named_scope("ph_G", _sid_G, False)
        _sid_S = nc.enter_named_scope("ph_S", False)[0]
        spsem = ["sp%d" % i for i in range(8)]
        m_pcs = {}
        m_pack = [None]

        def emit_s_vector(vector):
            vector.wait_ge(S["vv"], vv_g_end)
            sbufs = [s_buf, t2_buf]
            idx = 0
            for k in range(ntb):
                for c in range(KD):
                    if idx >= 8:
                        vector.wait_ge(S[spsem[idx % 8]], 16 * (idx // 8))
                    if idx >= 2:
                        vector.wait_ge(S["dv"], m_pcs[((idx - 2) // KD,
                                                      (idx - 2) % KD)])
                    vmark(vector.tensor_tensor_scan(
                        out=sbufs[idx % 2][:],
                        data0=aa_c[:, c:c + 1].broadcast_to([P, tt]),
                        data1=ug_ap(c, k * tt, tt),
                        initial=(0.0 if k == 0 else carry[:, c:c + 1]),
                        op0=ALU.mult, op1=ALU.add))
                    vwait(vector)
                    vector.tensor_copy(out=carry[:, c:c + 1],
                                       in_=sbufs[idx % 2][:, tt - 1:tt])
                    m_pcs[(k, c)] = vmark(vector.tensor_scalar_mul(
                        out=pcst[idx % 8][:],
                        in0=sbufs[idx % 2][:],
                        scalar1=cb_c[:, c:c + 1]))
                    idx += 1
            vector.wait_ge(S["dv"], m_pcs[(ntb - 1, KD - 1)])
            for c in range(KD):
                vector.reduce_sum(out=pack_sb[:, c:c + 1],
                                  in_=usum_sl[:, c * ntb:(c + 1) * ntb],
                                  axis=AX.X)
            m_pack[0] = vmark(vector.tensor_copy(out=pack_sb[:, KD:2 * KD],
                                                 in_=carry[:]))

        def emit_s_stores(scalar):
            idx = 0
            for k in range(ntb):
                for c in range(KD):
                    scalar.wait_ge(S["dv"], m_pcs[(k, c)])
                    scalar.dma_start(
                        out=pc_sc[c * P:(c + 1) * P, k * tt:(k + 1) * tt],
                        in_=pcst[idx % 8][:]).then_inc(S[spsem[idx % 8]], 16)
                    C.inc(spsem[idx % 8], 16)
                    idx += 1
            scalar.wait_ge(S["dv"], m_pack[0])
            scalar.dma_start(out=cc_in[:], in_=pack_sb[:]).then_inc(S["sc"], 16)
            C.inc("sc", 16)

        nc.leave_named_scope("ph_S", _sid_S, False)
        _sid_U = nc.enter_named_scope("ph_U", False)[0]
        es_G.close()
        es_U = ExitStack()
        wgp = [es_U.enter_context(nc.sbuf_tensor("wgpu%d" % i, [P, KD * P], BF16))
               for i in range(2)]
        wup = [es_U.enter_context(nc.sbuf_tensor("wupu%d" % i, [P, KD * P], BF16))
               for i in range(2)]
        silub = [es_U.enter_context(nc.sbuf_tensor("silu%d" % i, [P, 512], F32))
                 for i in range(2)]
        gstage = [es_U.enter_context(nc.sbuf_tensor("gst%d" % i, [P, 512], BF16))
                  for i in range(4)]

        sub_t = min(512, tcore)
        nsb = tcore // sub_t
        g = min(2, nsb)
        nst = nsb // g
        NU = HS * nst

        tt_u_base = C["tt"]
        ss_u_base = C["ss"]
        vv_u_base = C["vv"]
        wusem = ["wu0", "wu1"]
        sgsem = ["sg0", "sg1", "sg2", "sg3"]

        def emit_u_panels(gpsimd, hs_range):
            for hs in hs_range:
                if hs >= 2 and hs % 4 == 2:
                    ds = (hs - 2) // 4
                    gpsimd.dma_start(
                        out=wd_bf[ds].rearrange("p (k j) -> p k j", k=HS),
                        in_=wdr[ds].transpose([1, 0, 2]),
                    ).then_inc(S["wsc"], 16)
                    C.inc("wsc", 16)
                if hs < 2:
                    gpsimd.wait_ge(S["tt"], tt_g_base + KD)
                    gpsimd.wait_ge(S["vv"], vv_g_end)
                if hs >= 2:
                    gpsimd.wait_ge(S["tt"], tt_u_base + (hs - 1) * nst)
                gpsimd.dma_start(
                    out=wgp[hs % 2][:].rearrange("p (c j) -> p c j", c=KD),
                    in_=wg[hs].transpose([1, 0, 2])).then_inc(S[wusem[hs % 2]], 16)
                C.inc(wusem[hs % 2], 16)
                gpsimd.dma_start(
                    out=wup[hs % 2][:].rearrange("p (c j) -> p c j", c=KD),
                    in_=wu[hs].transpose([1, 0, 2])).then_inc(S[wusem[hs % 2]], 16)
                C.inc(wusem[hs % 2], 16)

        @block.gpsimd
        def _(gpsimd):
            emit_u_panels(gpsimd, range(0, 2))

        @block.gpsimd
        def _(gpsimd):
            emit_u_panels(gpsimd, range(2, 32))

        @block.gpsimd
        def _(gpsimd):
            gpsimd.wait_ge(S["sc"], 16)
            gpsimd.collective_compute(
                "AllGather", ALU.bypass,
                replica_groups=[list(range(NCORES))],
                ins=[cc_in[:]], outs=[cc_out[:]],
            ).then_inc(S["cc"], 1)
            C.inc("cc")

        @block.gpsimd
        def _(gpsimd):
            gpsimd.wait_ge(S["cc"], 1)
            gpsimd.dma_start(
                out=cc_all[:].rearrange("p (j s) -> p j s", j=NCORES),
                in_=cc_out[:].rearrange("(j p) s -> p j s", p=P),
            ).then_inc(S["ca"], 16)
            C.inc("ca", 16)
        m_ccload = C["ca"]

        @block.gpsimd
        def _(gpsimd):
            emit_u_panels(gpsimd, range(32, HS))

        @block.tensor
        def _(tensor):
            for hs in range(HS):
                tensor.wait_ge(S[wusem[hs % 2]], 32 * (hs // 2 + 1))
                for si in range(nst):
                    u = hs * nst + si
                    if u < 2:
                        gdone = KD - 2 + u
                        tensor.wait_ge(S["ss"], ss_g_base + (gdone + 1) * ntb)
                        tensor.wait_ge(S["vv"], vv_g_base + gdone + 1)
                    else:
                        tensor.wait_ge(S["ss"], ss_u_base + (u - 1) * g)
                        tensor.wait_ge(S["vv"], vv_u_base + (u - 1) * g)
                    grp = (u % 2) * 4
                    mm = None
                    for kd in range(KD):
                        for ib in range(g):
                            ts0 = (si * g + ib) * sub_t
                            tensor.matmul(
                                banks[grp + ib][:, 0:sub_t],
                                wgp[hs % 2][:, kd * P:(kd + 1) * P],
                                xn_ap(kd, ts0, sub_t),
                                start=(kd == 0), stop=(kd == KD - 1))
                        for ib in range(g):
                            ts0 = (si * g + ib) * sub_t
                            mm = tensor.matmul(
                                banks[grp + g + ib][:, 0:sub_t],
                                wup[hs % 2][:, kd * P:(kd + 1) * P],
                                xn_ap(kd, ts0, sub_t),
                                start=(kd == 0), stop=(kd == KD - 1))
                    mm.then_inc(S["tt"], 1)
                    C.inc("tt")

        def emit_u_silu(scalar, hs_range):
            for hs in hs_range:
                for si in range(nst):
                    u = hs * nst + si
                    scalar.wait_ge(S["tt"], tt_u_base + u + 1)
                    if u >= 1:
                        scalar.wait_ge(S["vv"], vv_u_base + u * g)
                    grp = (u % 2) * 4
                    for ib in range(g):
                        scalar.activation(
                            out=silub[ib][:, 0:sub_t],
                            in_=banks[grp + ib][:, 0:sub_t],
                            func=AF.Silu).then_inc(S["ss"], 1)
                        C.inc("ss")


        def emit_u_evac(vector, hs_range):
            for hs in hs_range:
                for si in range(nst):
                    u = hs * nst + si
                    grp = (u % 2) * 4
                    for ib in range(g):
                        gi = u * g + ib
                        vector.wait_ge(S["ss"], ss_u_base + gi + 1)
                        if gi >= 4:
                            vector.wait_ge(S[sgsem[gi % 4]], 16 * (gi // 4))
                        vector.tensor_mul(
                            out=gstage[gi % 4][:, 0:sub_t],
                            in0=silub[ib][:, 0:sub_t],
                            in1=banks[grp + g + ib][:, 0:sub_t],
                            ).then_inc(S["vv"], 1)
                        C.inc("vv")

        @block.vector
        def _(vector):
            emit_u_evac(vector, range(0, 8))

        @block.vector
        def _(vector):
            emit_s_vector(vector)

        @block.vector
        def _(vector):
            emit_u_evac(vector, range(8, 32))

        @block.vector
        def _(vector):
            vector.wait_ge(S["ca"], m_ccload)
            stride = 2 * KD
            _ca = cc_all[:]
            usv = bass.AP(_ca.tensor, _ca.offset,
                          [_ca.ap[0], [1, KD], [stride, NCORES]])
            vector.reduce_sum(out=pack_sb[:, 0:KD], in_=usv, axis=AX.X)
            lv = bass.AP(_ca.tensor, _ca.offset + KD,
                         [_ca.ap[0], [1, KD], [stride, NCORES]])
            mrv = mask_rep[:].rearrange("p (c j) -> p c j", c=KD)
            vector.tensor_mul(out=mlt_all[:].rearrange("p (c j) -> p c j", c=KD),
                              in0=mrv, in1=lv)
            _qm = qm1[:]
            qmv = bass.AP(_qm.tensor, _qm.offset,
                          [_qm.ap[0], [1, KD], [0, NCORES]])
            vmark(vector.tensor_mul(
                out=qeff_all[:].rearrange("p (c j) -> p c j", c=KD),
                in0=mrv, in1=qmv))
            vwait(vector)
            vmark(vector.tensor_scalar_add(out=qeff_all[:], in0=qeff_all[:],
                                           scalar1=1.0))
            vmark(vector.memset(Cc0[:], 0.0))
            vwait(vector)
            bufs = [Cc0, Cc1]
            for j in range(NCORES):
                _qa, _ma = qeff_all[:], mlt_all[:]
                qv = bass.AP(_qa.tensor, _qa.offset + j,
                             [_qa.ap[0], [NCORES, KD]])
                mv = bass.AP(_ma.tensor, _ma.offset + j,
                             [_ma.ap[0], [NCORES, KD]])
                vmark(vector.tensor_mul(out=bufs[(j + 1) % 2][:],
                                        in0=bufs[j % 2][:], in1=qv))
                vwait(vector)
                vmark(vector.tensor_add(out=bufs[(j + 1) % 2][:],
                                        in0=bufs[(j + 1) % 2][:], in1=mv))
                vwait(vector)
            vmark(vector.tensor_copy(out=hcar[:], in_=bufs[NCORES % 2][:]))
        m_usg_dv = dvn[0]

        @block.vector
        def _(vector):
            emit_u_evac(vector, range(32, HS))

        @block.scalar
        def _(scalar):
            emit_u_silu(scalar, range(0, 8))

        @block.scalar
        def _(scalar):
            emit_s_stores(scalar)

        @block.scalar
        def _(scalar):
            emit_u_silu(scalar, range(8, HS))

        @block.sync
        def _(sync):
            for hs in range(HS):
                for si in range(nst):
                    u = hs * nst + si
                    for ib in range(g):
                        gi = u * g + ib
                        sync.wait_ge(S["vv"], vv_u_base + gi + 1)
                        ts0 = (si * g + ib) * sub_t
                        sync.dma_start(
                            out=gated_sc[hs * P:(hs + 1) * P, ts0:ts0 + sub_t],
                            in_=gstage[gi % 4][:, 0:sub_t],
                        ).then_inc(S[sgsem[gi % 4]], 16)
                        C.inc(sgsem[gi % 4], 16)
        vv_u_end = C["vv"]
        ss_u_end = C["ss"]
        sg_totals = {n: C[n] for n in sgsem}
        sp_totals = {n: C[n] for n in spsem}

        nc.leave_named_scope("ph_U", _sid_U, False)
        _sid_CC = nc.enter_named_scope("ph_CC", False)[0]
        @block.tensor
        def _(tensor):
            tensor.wait_ge(S["dv"], m_usg_dv)
            tensor.wait_ge(S["ss"], ss_u_end)
            mm = None
            for c in range(KD):
                mm = tensor.matmul(banks[2][0:4, 0:1],
                                   U_sb[:, c * 4:(c + 1) * 4],
                                   pack_sb[:, c:c + 1],
                                   start=(c == 0), stop=(c == KD - 1))
            mm.then_inc(S["tt"], 1)
            C.inc("tt")
        m_p4 = C["tt"]

        @block.vector
        def _(vector):
            vector.wait_ge(S["tt"], m_p4)
            vector.tensor_copy(out=P4_sb[:],
                               in_=banks[2][0:4, 0:1]).then_inc(S["vv"], 1)
            C.inc("vv")
        m_p4v = C["vv"]

        @block.tensor
        def _(tensor):
            tensor.wait_ge(S["vv"], m_p4v)
            for c in range(KD):
                if c >= 2:
                    tensor.wait_ge(S["vv"], m_p4v + c - 1)
                tensor.matmul(banks[3 + (c % 2)][:, 0:1],
                              V_sb[0:4, c * P:(c + 1) * P], P4_sb[:],
                              start=True, stop=True).then_inc(S["tt"], 1)
                C.inc("tt")
        m_dlr_t = C["tt"]

        @block.vector
        def _(vector):
            sc = 1.0 / (float(t_total) * 4.0)
            for c in range(KD):
                vector.wait_ge(S["tt"], m_dlr_t - KD + c + 1)
                vector.tensor_scalar_mul(out=dlr_c[:, c:c + 1],
                                         in0=banks[3 + (c % 2)][:, 0:1],
                                         scalar1=sc).then_inc(S["vv"], 1)
                C.inc("vv")
            vector.wait_ge(S["vv"], C["vv"])
            vector.tensor_add(out=deff_c[:], in0=dlr_c[:],
                              in1=d_c[:]).then_inc(S["vv"], 1)
            C.inc("vv")
        m_dlr = C["vv"]

        es_U.close()
        es_xn.close()

        nc.leave_named_scope("ph_CC", _sid_CC, False)
        _sid_D = nc.enter_named_scope("ph_D", False)[0]
        es_D = ExitStack()
        dtt = min(512, tcore)
        ndtb = tcore // dtt
        HH = HS // 2
        wdp = [es_D.enter_context(
            nc.sbuf_tensor("wdp%d" % i, [P, HH * P], BF16)) for i in range(2)]
        gin = es_D.enter_context(nc.sbuf_tensor("gin", [P, HS * dtt], BF16))
        xdb = [es_D.enter_context(nc.sbuf_tensor("xdbuf%d" % i, [P, dtt], F32))
               for i in range(2)]
        pcdb = [es_D.enter_context(nc.sbuf_tensor("pcd%d" % i, [P, dtt], BF16))
                for i in range(2)]
        hbuf = es_D.enter_context(nc.sbuf_tensor("hbuf", [P, dtt], F32))
        fxb = [es_D.enter_context(nc.sbuf_tensor("fx%d" % i, [P, dtt], F32))
               for i in range(4)]
        osb = [es_D.enter_context(nc.sbuf_tensor("osd%d" % i, [P, dtt], F32))
               for i in range(2)]

        tt_d_base = C["tt"]
        tt_d = {}
        for tb in range(ndtb):
            for ds in range(KD):
                tt_d[(tb, ds)] = tt_d_base + 2 * (tb * KD + ds) + 2

        vv_d_base = C["vv"]
        d_order = []
        for tb in range(ndtb):
            for c in range(4):
                d_order.append(("fix", tb, c))
            for j in range(KD):
                d_order.append(("evac", tb, j))
                if j + 4 < KD:
                    d_order.append(("fix", tb, j + 4))
        fix_done = {}
        ev_done = {}
        for pos, (kind, tb, idx) in enumerate(d_order):
            if kind == "fix":
                fix_done[(tb, idx)] = vv_d_base + pos + 1
            else:
                ev_done[(tb, idx)] = vv_d_base + pos + 1

        NU_end_tt = tt_u_base + NU
        wdsem = ["wd0", "wd1"]
        xpsem = ["xp0", "xp1"]
        sosem = ["so0", "so1"]

        @block.gpsimd
        def _(gpsimd):
            gpsimd.wait_ge(S["tt"], NU_end_tt)
            gpsimd.wait_ge(S["vv"], vv_u_end)
            gpsimd.wait_ge(S["ss"], ss_u_end)
            gpsimd.wait_ge(S["wsc"], C["wsc"])
            for tb in range(ndtb):
                for ds in range(KD):
                    for hv in range(2):
                        i2 = 2 * (tb * KD + ds) + hv
                        if i2 >= 2:
                            gpsimd.wait_ge(S["tt"], tt_d_base + i2 - 1)
                        gpsimd.dma_start(
                            out=wdp[hv][:],
                            in_=wd_bf[ds][:, hv * HH * P:(hv + 1) * HH * P],
                        ).then_inc(S[wdsem[hv]], 16)
                        C.inc(wdsem[hv], 16)

        @block.sync
        def _(sync):
            sync.wait_ge(S["tt"], NU_end_tt)
            sync.wait_ge(S["vv"], vv_u_end)
            for n in sgsem:
                sync.wait_ge(S[n], sg_totals[n])
            for n in spsem:
                sync.wait_ge(S[n], sp_totals[n])
            for tb in range(ndtb):
                if tb >= 1:
                    sync.wait_ge(S["tt"], tt_d[(tb - 1, KD - 1)])
                for kh in range(HS):
                    sync.dma_start(
                        out=gin[:, kh * dtt:(kh + 1) * dtt],
                        in_=gated_sc[kh * P:(kh + 1) * P,
                                     tb * dtt:(tb + 1) * dtt],
                    ).then_inc(S["gi"], 16)
                    C.inc("gi", 16)
                for c in range(KD):
                    i = tb * KD + c
                    if i >= 2:
                        pv = divmod(i - 2, KD)
                        sync.wait_ge(S["vv"], fix_done[pv])
                    sync.dma_start(
                        out=xdb[i % 2][:],
                        in_=xT[c * P:(c + 1) * P, tb * dtt:(tb + 1) * dtt],
                    ).then_inc(S[xpsem[i % 2]], 16)
                    C.inc(xpsem[i % 2], 16)
                    sync.dma_start(
                        out=pcdb[i % 2][:],
                        in_=pc_sc[c * P:(c + 1) * P, tb * dtt:(tb + 1) * dtt],
                    ).then_inc(S[xpsem[i % 2]], 16)
                    C.inc(xpsem[i % 2], 16)

        @block.tensor
        def _(tensor):
            for tb in range(ndtb):
                tensor.wait_ge(S["gi"], 16 * HS * (tb + 1))
                for ds in range(KD):
                    i = tb * KD + ds
                    if i >= 2:
                        tensor.wait_ge(S["vv"],
                                       ev_done[((i - 2) // KD, (i - 2) % KD)])
                    elif i == 0:
                        tensor.wait_ge(S["vv"], vv_u_end)
                    mm = None
                    for hv in range(2):
                        tensor.wait_ge(S[wdsem[hv]], 16 * (tb * KD + ds + 1))
                        for khl in range(HH):
                            kh = hv * HH + khl
                            mm = tensor.matmul(
                                banks[i % 2][:, 0:dtt],
                                wdp[hv][:, khl * P:(khl + 1) * P],
                                gin[:, kh * dtt:(kh + 1) * dtt],
                                start=(kh == 0), stop=(kh == HS - 1))
                        mm.then_inc(S["tt"], 1)
                        C.inc("tt")
                    assert C["tt"] == tt_d[(tb, ds)], (C["tt"], tt_d[(tb, ds)])

        @block.vector
        def _(vector):
            vector.wait_ge(S["vv"], m_dlr)
            for kind, tb, idx in d_order:
                if kind == "fix":
                    c = idx
                    i = tb * KD + c
                    vector.wait_ge(S[xpsem[i % 2]], 32 * (i // 2 + 1))
                    if tb >= 1:
                        vector.wait_ge(S["vv"], fix_done[(tb - 1, c)])
                    if i >= 4:
                        vector.wait_ge(S["vv"],
                                       ev_done[((i - 4) // KD, (i - 4) % KD)])
                    fb = fxb[i % 4]
                    vector.scalar_tensor_tensor(
                        out=fb[:], in0=ug_ap(c, tb * dtt, dtt),
                        scalar=deff_c[:, c:c + 1], in1=xdb[i % 2][:],
                        op0=ALU.mult, op1=ALU.add)
                    vmark(vector.tensor_tensor_scan(
                        out=hbuf[:],
                        data0=aa_c[:, c:c + 1].broadcast_to([P, dtt]),
                        data1=zeros_tt[:, 0:dtt],
                        initial=hcar[:, c:c + 1],
                        op0=ALU.mult, op1=ALU.add))
                    vwait(vector)
                    vector.tensor_copy(out=hcar[:, c:c + 1],
                                       in_=hbuf[:, dtt - 1:dtt])
                    vmark(vector.scalar_tensor_tensor(
                        out=fb[:], in0=hbuf[:], scalar=cb_c[:, c:c + 1],
                        in1=fb[:], op0=ALU.mult, op1=ALU.add))
                    vwait(vector)
                    vector.tensor_add(out=fb[:], in0=fb[:],
                                      in1=pcdb[i % 2][:]).then_inc(S["vv"], 1)
                    assert C.inc("vv") == fix_done[(tb, c)]
                else:
                    ds = idx
                    i = tb * KD + ds
                    vector.wait_ge(S["tt"], tt_d[(tb, ds)])
                    if i >= 2:
                        vector.wait_ge(S[sosem[i % 2]], 16 * (i // 2))
                    vector.tensor_add(out=osb[i % 2][:],
                                      in0=banks[i % 2][:, 0:dtt],
                                      in1=fxb[i % 4][:]).then_inc(S["vv"], 1)
                    assert C.inc("vv") == ev_done[(tb, ds)]

        @block.scalar
        def _(scalar):
            for tb in range(ndtb):
                for ds in range(KD):
                    i = tb * KD + ds
                    scalar.wait_ge(S["vv"], ev_done[(tb, ds)])
                    scalar.dma_start(
                        out=outT[ds * P:(ds + 1) * P, tb * dtt:(tb + 1) * dtt],
                        in_=osb[i % 2][:]).then_inc(S[sosem[i % 2]], 16)
                    C.inc(sosem[i % 2], 16)

        @block.scalar
        def _(scalar):
            scalar.wait_ge(S["so0"], C["so0"])
            scalar.wait_ge(S["so1"], C["so1"])

        nc.leave_named_scope("ph_D", _sid_D, False)
        es_D.close()

    return nc


def _host_layouts(inputs, tcore):
    u = np.asarray(inputs["L0_attn_u"])
    xTfull = np.ascontiguousarray(u[0].T)
    gwb = np.ascontiguousarray(np.asarray(inputs["L0_attn_gate_w"])
                               .reshape(KD, P, KD, P).transpose(0, 2, 3, 1))
    wgb = np.ascontiguousarray(np.asarray(inputs["L0_ffn_w_gate"])
                               .reshape(HS, P, KD, P).transpose(0, 2, 3, 1))
    wub = np.ascontiguousarray(np.asarray(inputs["L0_ffn_w_up"])
                               .reshape(HS, P, KD, P).transpose(0, 2, 3, 1))
    wdb = np.ascontiguousarray(np.asarray(inputs["L0_ffn_w_down"])
                               .reshape(KD, P, HS, P).transpose(0, 2, 3, 1))
    col = lambda v: np.ascontiguousarray(np.asarray(v).reshape(KD, P).T)
    base = dict(gw=gwb, wg=wgb, wu=wub, wdr=wdb,
                avec=col(inputs["L0_attn_a"]), bvec=col(inputs["L0_attn_b"]),
                cvec=col(inputs["L0_attn_c"]), dvec=col(inputs["L0_attn_d"]),
                gbvec=col(inputs["L0_attn_gate_b"]),
                ulr=np.ascontiguousarray(
                    np.asarray(inputs["L0_attn_d_lr_u"]).reshape(KD, P, 4)),
                vlr=np.ascontiguousarray(np.asarray(inputs["L0_attn_d_lr_v"])))
    in_maps = []
    for i in range(NCORES):
        m = dict(base)
        m["xT"] = np.ascontiguousarray(xTfull[:, i * tcore:(i + 1) * tcore])
        mask = np.zeros((P, KD, NCORES), dtype=np.float32)
        mask[:, :, :i] = 1.0
        m["maskin"] = mask.reshape(P, KD * NCORES)
        in_maps.append(m)
    return in_maps


def kernel(**inputs):
    u = np.asarray(inputs["L0_attn_u"])
    T = u.shape[1]
    tcore = T // NCORES
    tt = min(512, tcore)
    in_maps = _host_layouts(inputs, tcore)
    nc = build_nc(tcore, tt, T)
    res = run_bass_kernel_spmd(nc, in_maps, core_ids=list(range(NCORES)))
    outT = np.concatenate([r["outT"] for r in res.results], axis=1)
    return np.ascontiguousarray(outT.T).reshape(1, T, D).astype(np.float32)


# version 07
# speedup vs baseline: 1.0239x; 1.0239x over previous
import numpy as np
import concourse.bass as bass
import concourse.mybir as mybir
from concourse.bass_utils import run_bass_kernel_spmd
from contextlib import ExitStack

F32 = mybir.dt.float32
BF16 = mybir.dt.bfloat16
AF = mybir.ActivationFunctionType
ALU = mybir.AluOpType
AX = mybir.AxisListType

P = 128
D = 2048
H = 8192
KD = D // P
HS = H // P
NCORES = 8

SEM_NAMES = ["tt", "vv", "ss", "cc",
             "xc", "xl0", "xl1", "gw0", "gw1", "wu0", "wu1", "ca", "gi",
             "xp0", "xp1", "wd0", "wd1",
             "sp0", "sp1", "sp2", "sp3", "sp4", "sp5", "sp6", "sp7",
             "sc", "sg0", "sg1", "sg2", "sg3", "so0", "so1",
             "wsc", "dv"]


class Cnt:
    def __init__(self, names):
        self.v = {n: 0 for n in names}

    def inc(self, n, k=1):
        self.v[n] += k
        return self.v[n]

    def __getitem__(self, n):
        return self.v[n]


def build_nc(tcore, tt, t_total):
    ntb = tcore // tt
    assert tcore % tt == 0 and tt <= 512
    assert ntb <= 4
    ltt = min(512, tt)
    lnt = tcore // ltt

    nc = bass.Bass("TRN2", target_bir_lowering=False, debug=False)

    xT = nc.dram_tensor("xT", [D, tcore], F32, kind="ExternalInput").ap()
    gw = nc.dram_tensor("gw", [KD, KD, P, P], F32, kind="ExternalInput").ap()
    wg = nc.dram_tensor("wg", [HS, KD, P, P], F32, kind="ExternalInput").ap()
    wu = nc.dram_tensor("wu", [HS, KD, P, P], F32, kind="ExternalInput").ap()
    wdr = nc.dram_tensor("wdr", [KD, HS, P, P], F32, kind="ExternalInput").ap()
    avec = nc.dram_tensor("avec", [P, KD], F32, kind="ExternalInput").ap()
    bvec = nc.dram_tensor("bvec", [P, KD], F32, kind="ExternalInput").ap()
    cvec = nc.dram_tensor("cvec", [P, KD], F32, kind="ExternalInput").ap()
    dvec = nc.dram_tensor("dvec", [P, KD], F32, kind="ExternalInput").ap()
    gbvec = nc.dram_tensor("gbvec", [P, KD], F32, kind="ExternalInput").ap()
    ulr = nc.dram_tensor("ulr", [KD, P, 4], F32, kind="ExternalInput").ap()
    vlr = nc.dram_tensor("vlr", [4, D], F32, kind="ExternalInput").ap()
    maskin = nc.dram_tensor("maskin", [P, KD * NCORES], F32, kind="ExternalInput").ap()
    outT = nc.dram_tensor("outT", [D, tcore], F32, kind="ExternalOutput").ap()

    gated_sc = nc.dram_tensor("gated_sc", [H, tcore], BF16).ap()
    wd_bf = nc.dram_tensor("wd_bf", [KD, P, HS * P], BF16).ap()
    pc_sc = nc.dram_tensor("pc_sc", [D, tcore], BF16).ap()
    cc_in = nc.dram_tensor("cc_in", [P, 2 * KD], F32).ap()
    cc_out = nc.dram_tensor("cc_out", [NCORES * P, 2 * KD], F32).ap()

    C = Cnt(SEM_NAMES)

    with ExitStack() as ctx:
        S = {n: ctx.enter_context(nc.semaphore(n)) for n in SEM_NAMES}

        sb = lambda name, shape, dt: ctx.enter_context(nc.sbuf_tensor(name, shape, dt))
        dvn = [0]

        def vmark(inst):
            inst.then_inc(S["dv"], 1)
            dvn[0] += 1
            return dvn[0]

        def vwait(vector):
            vector.wait_ge(S["dv"], dvn[0])
        a_c = sb("a_c", [P, KD], F32)
        b_c = sb("b_c", [P, KD], F32)
        c_c = sb("c_c", [P, KD], F32)
        d_c = sb("d_c", [P, KD], F32)
        gb_c = sb("gb_c", [P, KD], F32)
        aa_c = sb("aa_c", [P, KD], F32)
        cb_c = sb("cb_c", [P, KD], F32)
        qm1 = sb("qm1", [P, KD], F32)
        dlr_c = sb("dlr_c", [P, KD], F32)
        deff_c = sb("deff_c", [P, KD], F32)
        carry = sb("carry", [P, KD], F32)
        hcar = sb("hcar", [P, KD], F32)
        Cc0 = sb("Cc0", [P, KD], F32)
        Cc1 = sb("Cc1", [P, KD], F32)
        usum_sl = sb("usum_sl", [P, KD * ntb], F32)
        pack_sb = sb("pack_sb", [P, 2 * KD], F32)
        mask_rep = sb("mask_rep", [P, KD * NCORES], F32)
        qeff_all = sb("qeff_all", [P, KD * NCORES], F32)
        mlt_all = sb("mlt_all", [P, KD * NCORES], F32)
        U_sb = sb("U_sb", [P, KD * 4], F32)
        V_sb = sb("V_sb", [4, D], F32)
        P4_sb = sb("P4_sb", [4, 1], F32)
        cc_all = sb("cc_all", [P, NCORES * 2 * KD], F32)
        ones_col = sb("ones_col", [P, 1], F32)
        ones_row = sb("ones_row", [1, P], F32)
        zeros_tt = sb("zeros_tt", [P, 512], F32)

        s_buf = sb("s_buf", [P, tt], F32)
        t2_buf = sb("t2_buf", [P, tt], F32)
        pcst = [sb("pcst%d" % i, [P, tt], BF16) for i in range(8)]

        es_xn = ExitStack()
        xn_bf = es_xn.enter_context(
            nc.sbuf_tensor("xn_bf", [P, KD * tcore], BF16, side="right"))
        ug_holder = {}

        banks = [ctx.enter_context(nc.psum_tensor("pb%d" % i, [P, 512], F32))
                 for i in range(8)]

        block = ctx.enter_context(nc.Block())

        def xn_ap(c, t0, n):
            return xn_bf[:, c * tcore + t0: c * tcore + t0 + n]

        def ug_ap(c, t0, n):
            ug_bf = ug_holder["t"]
            return ug_bf[:, c * tcore + t0: c * tcore + t0 + n]

        _sid_setup = nc.enter_named_scope("ph_setup", False)[0]
        @block.sync
        def _(sync):
            for src, dst in [(avec, a_c), (bvec, b_c), (cvec, c_c), (dvec, d_c),
                             (gbvec, gb_c), (maskin, mask_rep), (vlr, V_sb)]:
                sync.dma_start(out=dst[:], in_=src[:]).then_inc(S["xc"], 16)
                C.inc("xc", 16)
            sync.dma_start(out=U_sb[:].rearrange("p (c r) -> p c r", c=KD),
                           in_=ulr[:].transpose([1, 0, 2])).then_inc(S["xc"], 16)
            C.inc("xc", 16)
        m_consts = C["xc"]

        @block.scalar
        def _(scalar):
            scalar.wait_ge(S["xc"], m_consts)
            scalar.activation(out=aa_c[:], in_=a_c[:],
                              func=AF.Tanh).then_inc(S["ss"], 1)
            C.inc("ss")
        m_aa = C["ss"]

        @block.vector
        def _(vector):
            vector.memset(ones_col[:], 1.0)
            vector.memset(ones_row[:], 1.0)
            vector.memset(zeros_tt[:], 0.0)
            vector.memset(carry[:], 0.0)
            vector.memset(usum_sl[:], 0.0)
            vector.wait_ge(S["xc"], m_consts)
            vector.wait_ge(S["ss"], m_aa)
            vector.tensor_mul(out=cb_c[:], in0=c_c[:], in1=b_c[:])
            vmark(vector.tensor_mul(out=qm1[:], in0=aa_c[:], in1=aa_c[:]))
            nsq = int(round(np.log2(tcore)))
            assert 2 ** nsq == tcore
            for _i in range(nsq - 1):
                vwait(vector)
                vmark(vector.tensor_mul(out=qm1[:], in0=qm1[:], in1=qm1[:]))
            vwait(vector)
            vmark(vector.tensor_scalar_add(out=qm1[:], in0=qm1[:], scalar1=-1.0))
            vwait(vector)
            vector.engine_nop().then_inc(S["vv"], 1)
            C.inc("vv")
        m_setup_v = C["vv"]

        nc.leave_named_scope("ph_setup", _sid_setup, False)
        _sid_L = nc.enter_named_scope("ph_L", False)[0]
        es_L = ExitStack()
        x_sbs = [es_L.enter_context(
            nc.sbuf_tensor("x_sb%d" % i, [P, KD * ltt], F32, side="right"))
            for i in range(2)]
        sq_sb = es_L.enter_context(
            nc.sbuf_tensor("sq_sb", [P, KD * ltt], F32, side="right"))
        t1x = [es_L.enter_context(
            nc.sbuf_tensor("t1x%d" % i, [P, ltt], F32, side="right"))
            for i in range(2)]
        srows = es_L.enter_context(
            nc.sbuf_tensor("srows", [1, 7 * ltt], F32, side="right"))
        mrow = srows[:, 0 * ltt:1 * ltt]
        e2row = srows[:, 1 * ltt:2 * ltt]
        vrow = srows[:, 2 * ltt:3 * ltt]
        r0row = srows[:, 3 * ltt:4 * ltt]
        t1row = srows[:, 4 * ltt:5 * ltt]
        rstdrow = srows[:, 5 * ltt:6 * ltt]
        nmrow = srows[:, 6 * ltt:7 * ltt]

        xn_done = [None] * lnt
        stats_done = [None] * lnt
        m_invs = [None] * lnt

        for k in range(lnt):
            x_sb = x_sbs[k % 2]
            xls = "xl%d" % (k % 2)

            @block.sync
            def _(sync):
                if k >= 2:
                    sync.wait_ge(S["vv"], xn_done[k - 2])
                for c in range(KD):
                    sync.dma_start(
                        out=x_sb[:, c * ltt:(c + 1) * ltt],
                        in_=xT[c * P:(c + 1) * P, k * ltt:(k + 1) * ltt],
                    ).then_inc(S[xls], 16)
                    C.inc(xls, 16)
            m_xl = C[xls]

            ss_base = C["ss"]

            @block.scalar
            def _(scalar):
                if k >= 1:
                    scalar.wait_ge(S["tt"], stats_done[k - 1])
                scalar.wait_ge(S[xls], m_xl)
                for c in range(KD):
                    scalar.activation(
                        out=sq_sb[:, c * ltt:(c + 1) * ltt],
                        in_=x_sb[:, c * ltt:(c + 1) * ltt],
                        func=AF.Square).then_inc(S["ss"], 1)
                    C.inc("ss")

            @block.tensor
            def _(tensor):
                if k >= 1:
                    tensor.wait_ge(S["vv"], m_invs[k - 1])
                else:
                    tensor.wait_ge(S["vv"], m_setup_v)
                mm = None
                for c in range(KD):
                    tensor.wait_ge(S["ss"], ss_base + c + 1)
                    tensor.matmul(banks[0][0:1, 0:ltt], ones_col[:],
                                  x_sb[:, c * ltt:(c + 1) * ltt],
                                  start=(c == 0), stop=(c == KD - 1))
                    mm = tensor.matmul(banks[1][0:1, 0:ltt], ones_col[:],
                                       sq_sb[:, c * ltt:(c + 1) * ltt],
                                       start=(c == 0), stop=(c == KD - 1))
                mm.then_inc(S["tt"], 1)
                stats_done[k] = C.inc("tt")

            @block.vector
            def _(vector):
                vector.wait_ge(S["tt"], stats_done[k])
                inv = 1.0 / D
                vector.tensor_scalar_mul(out=mrow, in0=banks[0][0:1, 0:ltt],
                                         scalar1=inv)
                vmark(vector.tensor_scalar_mul(out=e2row, in0=banks[1][0:1, 0:ltt],
                                               scalar1=inv))
                vwait(vector)
                vmark(vector.tensor_mul(out=t1row, in0=mrow, in1=mrow))
                vwait(vector)
                vmark(vector.tensor_sub(out=vrow, in0=e2row, in1=t1row))
                vwait(vector)
                vmark(vector.tensor_scalar_add(out=vrow, in0=vrow, scalar1=1e-5))
                vwait(vector)
                vector.reciprocal(out=r0row, in_=vrow).then_inc(S["vv"], 1)
                m_invs[k] = C.inc("vv")
            m_inv = C["vv"]

            @block.scalar
            def _(scalar):
                scalar.wait_ge(S["vv"], m_inv)
                scalar.activation(out=r0row, in_=r0row,
                                  func=AF.Sqrt).then_inc(S["ss"], 1)
                C.inc("ss")
            m_sqrt = C["ss"]

            @block.vector
            def _(vector):
                vector.wait_ge(S["ss"], m_sqrt)
                vmark(vector.tensor_mul(out=t1row, in0=r0row, in1=r0row))
                vwait(vector)
                vmark(vector.tensor_mul(out=t1row, in0=t1row, in1=vrow))
                vwait(vector)
                vmark(vector.tensor_scalar(out=t1row, in0=t1row, scalar1=-0.5,
                                           scalar2=1.5, op0=ALU.mult, op1=ALU.add))
                vwait(vector)
                vmark(vector.tensor_mul(out=rstdrow, in0=r0row, in1=t1row))
                vwait(vector)
                vmark(vector.tensor_mul(out=nmrow, in0=mrow, in1=rstdrow))
                vwait(vector)
                vector.tensor_scalar_mul(out=nmrow, in0=nmrow,
                                         scalar1=-1.0).then_inc(S["vv"], 1)
                C.inc("vv")
            m_rows = C["vv"]

            @block.tensor
            def _(tensor):
                tensor.wait_ge(S["vv"], m_rows)
                tensor.matmul(banks[2][:, 0:ltt], ones_row[:], rstdrow,
                              start=True, stop=True)
                tensor.matmul(banks[3][:, 0:ltt], ones_row[:], nmrow,
                              start=True, stop=True).then_inc(S["tt"], 1)
                C.inc("tt")
            m_bc = C["tt"]

            @block.vector
            def _(vector):
                vector.wait_ge(S["tt"], m_bc)
                last = None
                for c in range(KD):
                    if c >= 2:
                        vwait(vector)
                    vmark(vector.tensor_mul(out=t1x[c % 2][:],
                                            in0=x_sb[:, c * ltt:(c + 1) * ltt],
                                            in1=banks[2][:, 0:ltt]))
                    vwait(vector)
                    last = vector.tensor_add(out=xn_ap(c, k * ltt, ltt),
                                             in0=t1x[c % 2][:],
                                             in1=banks[3][:, 0:ltt])
                    if c < KD - 1:
                        vmark(last)
                last.then_inc(S["vv"], 1)
                xn_done[k] = C.inc("vv")

        nc.leave_named_scope("ph_L", _sid_L, False)
        _sid_G = nc.enter_named_scope("ph_G", False)[0]
        es_L.close()
        ug_holder["t"] = ctx.enter_context(
            nc.sbuf_tensor("ug_bf", [P, KD * tcore], BF16))
        es_G = ExitStack()
        gw_pan = [es_G.enter_context(nc.sbuf_tensor("gwp%d" % i, [P, KD * P], BF16))
                  for i in range(2)]
        gs_bufs = [es_G.enter_context(nc.sbuf_tensor("gs%d" % i, [P, tt], F32))
                   for i in range(ntb)]

        tt_g_base = C["tt"]
        ss_g_base = C["ss"]
        vv_g_base = C["vv"]
        gwsem = ["gw0", "gw1"]

        @block.gpsimd
        def _(gpsimd):
            gpsimd.wait_ge(S["vv"], xn_done[lnt - 1])
            for es in range(KD):
                if es >= 2:
                    gpsimd.wait_ge(S["tt"], tt_g_base + es - 1)
                gpsimd.dma_start(
                    out=gw_pan[es % 2][:].rearrange("p (c j) -> p c j", c=KD),
                    in_=gw[es].transpose([1, 0, 2])).then_inc(S[gwsem[es % 2]], 16)
                C.inc(gwsem[es % 2], 16)

        @block.tensor
        def _(tensor):
            tensor.wait_ge(S["vv"], xn_done[lnt - 1])
            for es in range(KD):
                tensor.wait_ge(S[gwsem[es % 2]], 16 * (es // 2 + 1))
                if es >= 2:
                    tensor.wait_ge(S["ss"], ss_g_base + (es - 1) * ntb)
                grp = (es % 2) * 4
                mm = None
                for kd in range(KD):
                    for tb in range(ntb):
                        mm = tensor.matmul(
                            banks[grp + tb][:, 0:tt],
                            gw_pan[es % 2][:, kd * P:(kd + 1) * P],
                            xn_ap(kd, tb * tt, tt),
                            start=(kd == 0), stop=(kd == KD - 1))
                mm.then_inc(S["tt"], 1)
                C.inc("tt")

        @block.scalar
        def _(scalar):
            for es in range(KD):
                scalar.wait_ge(S["tt"], tt_g_base + es + 1)
                if es >= 1:
                    scalar.wait_ge(S["vv"], vv_g_base + es)
                grp = (es % 2) * 4
                for tb in range(ntb):
                    scalar.activation(
                        out=gs_bufs[tb][:], in_=banks[grp + tb][:, 0:tt],
                        func=AF.Sigmoid, bias=gb_c[:, es:es + 1],
                    ).then_inc(S["ss"], 1)
                    C.inc("ss")

        @block.vector
        def _(vector):
            for es in range(KD):
                for tb in range(ntb):
                    vector.wait_ge(S["ss"], ss_g_base + es * ntb + tb + 1)
                    inst = vector.scalar_tensor_tensor(
                        out=ug_ap(es, tb * tt, tt),
                        in0=gs_bufs[tb][:], scalar=1.0,
                        in1=xn_ap(es, tb * tt, tt),
                        op0=ALU.mult, op1=ALU.mult,
                        accum_out=usum_sl[:, es * ntb + tb:es * ntb + tb + 1])
                inst.then_inc(S["vv"], 1)
                C.inc("vv")
        vv_g_end = C["vv"]
        ss_g_end = C["ss"]

        nc.leave_named_scope("ph_G", _sid_G, False)
        _sid_S = nc.enter_named_scope("ph_S", False)[0]
        spsem = ["sp%d" % i for i in range(8)]
        m_pcs = {}
        m_pack = [None]

        def make_s_chunk_emitter():
            sbufs = [s_buf, t2_buf]
            state = {"idx": 0}

            def emit_one(vector):
                idx = state["idx"]
                nchunks = ntb * KD
                if idx > nchunks:
                    return
                if idx == nchunks:
                    vector.wait_ge(S["dv"], m_pcs[(ntb - 1, KD - 1)])
                    for c in range(KD):
                        vector.reduce_sum(out=pack_sb[:, c:c + 1],
                                          in_=usum_sl[:, c * ntb:(c + 1) * ntb],
                                          axis=AX.X)
                    m_pack[0] = vmark(vector.tensor_copy(
                        out=pack_sb[:, KD:2 * KD], in_=carry[:]))
                    state["idx"] += 1
                    return
                k, c = divmod(idx, KD)
                if idx == 0:
                    vector.wait_ge(S["vv"], vv_g_end)
                if idx >= 8:
                    vector.wait_ge(S[spsem[idx % 8]], 16 * (idx // 8))
                if idx >= 2:
                    vector.wait_ge(S["dv"], m_pcs[((idx - 2) // KD,
                                                  (idx - 2) % KD)])
                vmark(vector.tensor_tensor_scan(
                    out=sbufs[idx % 2][:],
                    data0=aa_c[:, c:c + 1].broadcast_to([P, tt]),
                    data1=ug_ap(c, k * tt, tt),
                    initial=(0.0 if k == 0 else carry[:, c:c + 1]),
                    op0=ALU.mult, op1=ALU.add))
                vwait(vector)
                vector.tensor_copy(out=carry[:, c:c + 1],
                                   in_=sbufs[idx % 2][:, tt - 1:tt])
                m_pcs[(k, c)] = vmark(vector.tensor_scalar_mul(
                    out=pcst[idx % 8][:],
                    in0=sbufs[idx % 2][:],
                    scalar1=cb_c[:, c:c + 1]))
                state["idx"] += 1

            return emit_one, state

        def make_s_store_emitter():
            state = {"idx": 0}

            def emit_one(sync):
                idx = state["idx"]
                nchunks = ntb * KD
                if idx > nchunks:
                    return
                if idx == nchunks:
                    sync.wait_ge(S["dv"], m_pack[0])
                    sync.dma_start(out=cc_in[:],
                                   in_=pack_sb[:]).then_inc(S["sc"], 16)
                    C.inc("sc", 16)
                    state["idx"] += 1
                    return
                k, c = divmod(idx, KD)
                sync.wait_ge(S["dv"], m_pcs[(k, c)])
                sync.dma_start(
                    out=pc_sc[c * P:(c + 1) * P, k * tt:(k + 1) * tt],
                    in_=pcst[idx % 8][:]).then_inc(S[spsem[idx % 8]], 16)
                C.inc(spsem[idx % 8], 16)
                state["idx"] += 1

            return emit_one, state

        nc.leave_named_scope("ph_S", _sid_S, False)
        _sid_U = nc.enter_named_scope("ph_U", False)[0]
        es_G.close()
        es_U = ExitStack()
        wgp = [es_U.enter_context(nc.sbuf_tensor("wgpu%d" % i, [P, KD * P], BF16))
               for i in range(2)]
        wup = [es_U.enter_context(nc.sbuf_tensor("wupu%d" % i, [P, KD * P], BF16))
               for i in range(2)]
        silub = [es_U.enter_context(nc.sbuf_tensor("silu%d" % i, [P, 512], F32))
                 for i in range(2)]
        gstage = [es_U.enter_context(nc.sbuf_tensor("gst%d" % i, [P, 512], BF16))
                  for i in range(4)]

        sub_t = min(512, tcore)
        nsb = tcore // sub_t
        g = min(2, nsb)
        nst = nsb // g
        NU = HS * nst

        tt_u_base = C["tt"]
        ss_u_base = C["ss"]
        vv_u_base = C["vv"]
        wusem = ["wu0", "wu1"]
        sgsem = ["sg0", "sg1", "sg2", "sg3"]

        def emit_u_panels(gpsimd, hs_range):
            for hs in hs_range:
                if hs >= 2 and hs % 4 == 2:
                    ds = (hs - 2) // 4
                    gpsimd.dma_start(
                        out=wd_bf[ds].rearrange("p (k j) -> p k j", k=HS),
                        in_=wdr[ds].transpose([1, 0, 2]),
                    ).then_inc(S["wsc"], 16)
                    C.inc("wsc", 16)
                if hs < 2:
                    gpsimd.wait_ge(S["tt"], tt_g_base + KD)
                    gpsimd.wait_ge(S["vv"], vv_g_end)
                if hs >= 2:
                    gpsimd.wait_ge(S["tt"], tt_u_base + (hs - 1) * nst)
                gpsimd.dma_start(
                    out=wgp[hs % 2][:].rearrange("p (c j) -> p c j", c=KD),
                    in_=wg[hs].transpose([1, 0, 2])).then_inc(S[wusem[hs % 2]], 16)
                C.inc(wusem[hs % 2], 16)
                gpsimd.dma_start(
                    out=wup[hs % 2][:].rearrange("p (c j) -> p c j", c=KD),
                    in_=wu[hs].transpose([1, 0, 2])).then_inc(S[wusem[hs % 2]], 16)
                C.inc(wusem[hs % 2], 16)

        @block.gpsimd
        def _(gpsimd):
            emit_u_panels(gpsimd, range(0, 2))

        @block.gpsimd
        def _(gpsimd):
            emit_u_panels(gpsimd, range(2, 32))

        @block.gpsimd
        def _(gpsimd):
            gpsimd.wait_ge(S["sc"], 16)
            gpsimd.collective_compute(
                "AllGather", ALU.bypass,
                replica_groups=[list(range(NCORES))],
                ins=[cc_in[:]], outs=[cc_out[:]],
            ).then_inc(S["cc"], 1)
            C.inc("cc")

        @block.gpsimd
        def _(gpsimd):
            gpsimd.wait_ge(S["cc"], 1)
            gpsimd.dma_start(
                out=cc_all[:].rearrange("p (j s) -> p j s", j=NCORES),
                in_=cc_out[:].rearrange("(j p) s -> p j s", p=P),
            ).then_inc(S["ca"], 16)
            C.inc("ca", 16)
        m_ccload = C["ca"]

        @block.gpsimd
        def _(gpsimd):
            emit_u_panels(gpsimd, range(32, HS))

        @block.tensor
        def _(tensor):
            for hs in range(HS):
                tensor.wait_ge(S[wusem[hs % 2]], 32 * (hs // 2 + 1))
                for si in range(nst):
                    u = hs * nst + si
                    if u < 2:
                        gdone = KD - 2 + u
                        tensor.wait_ge(S["ss"], ss_g_base + (gdone + 1) * ntb)
                        tensor.wait_ge(S["vv"], vv_g_base + gdone + 1)
                    else:
                        tensor.wait_ge(S["ss"], ss_u_base + (u - 1) * g)
                        tensor.wait_ge(S["vv"], vv_u_base + (u - 1) * g)
                    grp = (u % 2) * 4
                    mm = None
                    for kd in range(KD):
                        for ib in range(g):
                            ts0 = (si * g + ib) * sub_t
                            tensor.matmul(
                                banks[grp + ib][:, 0:sub_t],
                                wgp[hs % 2][:, kd * P:(kd + 1) * P],
                                xn_ap(kd, ts0, sub_t),
                                start=(kd == 0), stop=(kd == KD - 1))
                        for ib in range(g):
                            ts0 = (si * g + ib) * sub_t
                            mm = tensor.matmul(
                                banks[grp + g + ib][:, 0:sub_t],
                                wup[hs % 2][:, kd * P:(kd + 1) * P],
                                xn_ap(kd, ts0, sub_t),
                                start=(kd == 0), stop=(kd == KD - 1))
                    mm.then_inc(S["tt"], 1)
                    C.inc("tt")

        def emit_u_silu(scalar, hs_range):
            for hs in hs_range:
                for si in range(nst):
                    u = hs * nst + si
                    scalar.wait_ge(S["tt"], tt_u_base + u + 1)
                    if u >= 1:
                        scalar.wait_ge(S["vv"], vv_u_base + u * g)
                    grp = (u % 2) * 4
                    for ib in range(g):
                        scalar.activation(
                            out=silub[ib][:, 0:sub_t],
                            in_=banks[grp + ib][:, 0:sub_t],
                            func=AF.Silu).then_inc(S["ss"], 1)
                        C.inc("ss")


        s_chunk, s_state = make_s_chunk_emitter()
        s_store, s_store_state = make_s_store_emitter()

        def emit_u_evac(vector, hs_range):
            for hs in hs_range:
                for si in range(nst):
                    u = hs * nst + si
                    grp = (u % 2) * 4
                    for ib in range(g):
                        gi = u * g + ib
                        vector.wait_ge(S["ss"], ss_u_base + gi + 1)
                        if gi >= 4:
                            vector.wait_ge(S[sgsem[gi % 4]], 16 * (gi // 4))
                        vector.tensor_mul(
                            out=gstage[gi % 4][:, 0:sub_t],
                            in0=silub[ib][:, 0:sub_t],
                            in1=banks[grp + g + ib][:, 0:sub_t],
                            ).then_inc(S["vv"], 1)
                        C.inc("vv")
                        s_chunk(vector)

        @block.vector
        def _(vector):
            emit_u_evac(vector, range(0, 32))

        @block.vector
        def _(vector):
            vector.wait_ge(S["ca"], m_ccload)
            stride = 2 * KD
            _ca = cc_all[:]
            usv = bass.AP(_ca.tensor, _ca.offset,
                          [_ca.ap[0], [1, KD], [stride, NCORES]])
            vector.reduce_sum(out=pack_sb[:, 0:KD], in_=usv, axis=AX.X)
            lv = bass.AP(_ca.tensor, _ca.offset + KD,
                         [_ca.ap[0], [1, KD], [stride, NCORES]])
            mrv = mask_rep[:].rearrange("p (c j) -> p c j", c=KD)
            vector.tensor_mul(out=mlt_all[:].rearrange("p (c j) -> p c j", c=KD),
                              in0=mrv, in1=lv)
            _qm = qm1[:]
            qmv = bass.AP(_qm.tensor, _qm.offset,
                          [_qm.ap[0], [1, KD], [0, NCORES]])
            vmark(vector.tensor_mul(
                out=qeff_all[:].rearrange("p (c j) -> p c j", c=KD),
                in0=mrv, in1=qmv))
            vwait(vector)
            vmark(vector.tensor_scalar_add(out=qeff_all[:], in0=qeff_all[:],
                                           scalar1=1.0))
            vmark(vector.memset(Cc0[:], 0.0))
            vwait(vector)
            bufs = [Cc0, Cc1]
            for j in range(NCORES):
                _qa, _ma = qeff_all[:], mlt_all[:]
                qv = bass.AP(_qa.tensor, _qa.offset + j,
                             [_qa.ap[0], [NCORES, KD]])
                mv = bass.AP(_ma.tensor, _ma.offset + j,
                             [_ma.ap[0], [NCORES, KD]])
                vmark(vector.tensor_mul(out=bufs[(j + 1) % 2][:],
                                        in0=bufs[j % 2][:], in1=qv))
                vwait(vector)
                vmark(vector.tensor_add(out=bufs[(j + 1) % 2][:],
                                        in0=bufs[(j + 1) % 2][:], in1=mv))
                vwait(vector)
            vmark(vector.tensor_copy(out=hcar[:], in_=bufs[NCORES % 2][:]))
        m_usg_dv = dvn[0]

        @block.vector
        def _(vector):
            emit_u_evac(vector, range(32, HS))

        @block.scalar
        def _(scalar):
            emit_u_silu(scalar, range(0, HS))

        @block.sync
        def _(sync):
            for hs in range(HS):
                for si in range(nst):
                    u = hs * nst + si
                    for ib in range(g):
                        gi = u * g + ib
                        sync.wait_ge(S["vv"], vv_u_base + gi + 1)
                        ts0 = (si * g + ib) * sub_t
                        sync.dma_start(
                            out=gated_sc[hs * P:(hs + 1) * P, ts0:ts0 + sub_t],
                            in_=gstage[gi % 4][:, 0:sub_t],
                        ).then_inc(S[sgsem[gi % 4]], 16)
                        C.inc(sgsem[gi % 4], 16)
                        if gi >= 1:
                            s_store(sync)
            while s_store_state["idx"] <= ntb * KD:
                s_store(sync)
        vv_u_end = C["vv"]
        ss_u_end = C["ss"]
        sg_totals = {n: C[n] for n in sgsem}
        sp_totals = {n: C[n] for n in spsem}

        nc.leave_named_scope("ph_U", _sid_U, False)
        _sid_CC = nc.enter_named_scope("ph_CC", False)[0]
        @block.tensor
        def _(tensor):
            tensor.wait_ge(S["dv"], m_usg_dv)
            tensor.wait_ge(S["ss"], ss_u_end)
            mm = None
            for c in range(KD):
                mm = tensor.matmul(banks[2][0:4, 0:1],
                                   U_sb[:, c * 4:(c + 1) * 4],
                                   pack_sb[:, c:c + 1],
                                   start=(c == 0), stop=(c == KD - 1))
            mm.then_inc(S["tt"], 1)
            C.inc("tt")
        m_p4 = C["tt"]

        @block.vector
        def _(vector):
            vector.wait_ge(S["tt"], m_p4)
            vector.tensor_copy(out=P4_sb[:],
                               in_=banks[2][0:4, 0:1]).then_inc(S["vv"], 1)
            C.inc("vv")
        m_p4v = C["vv"]

        @block.tensor
        def _(tensor):
            tensor.wait_ge(S["vv"], m_p4v)
            for c in range(KD):
                if c >= 2:
                    tensor.wait_ge(S["vv"], m_p4v + c - 1)
                tensor.matmul(banks[3 + (c % 2)][:, 0:1],
                              V_sb[0:4, c * P:(c + 1) * P], P4_sb[:],
                              start=True, stop=True).then_inc(S["tt"], 1)
                C.inc("tt")
        m_dlr_t = C["tt"]

        @block.vector
        def _(vector):
            sc = 1.0 / (float(t_total) * 4.0)
            for c in range(KD):
                vector.wait_ge(S["tt"], m_dlr_t - KD + c + 1)
                vector.tensor_scalar_mul(out=dlr_c[:, c:c + 1],
                                         in0=banks[3 + (c % 2)][:, 0:1],
                                         scalar1=sc).then_inc(S["vv"], 1)
                C.inc("vv")
            vector.wait_ge(S["vv"], C["vv"])
            vector.tensor_add(out=deff_c[:], in0=dlr_c[:],
                              in1=d_c[:]).then_inc(S["vv"], 1)
            C.inc("vv")
        m_dlr = C["vv"]

        es_U.close()
        es_xn.close()

        nc.leave_named_scope("ph_CC", _sid_CC, False)
        _sid_D = nc.enter_named_scope("ph_D", False)[0]
        es_D = ExitStack()
        dtt = min(512, tcore)
        ndtb = tcore // dtt
        HH = HS // 2
        wdp = [es_D.enter_context(
            nc.sbuf_tensor("wdp%d" % i, [P, HH * P], BF16)) for i in range(2)]
        gin = es_D.enter_context(nc.sbuf_tensor("gin", [P, HS * dtt], BF16))
        xdb = [es_D.enter_context(nc.sbuf_tensor("xdbuf%d" % i, [P, dtt], F32))
               for i in range(2)]
        pcdb = [es_D.enter_context(nc.sbuf_tensor("pcd%d" % i, [P, dtt], BF16))
                for i in range(2)]
        hbuf = es_D.enter_context(nc.sbuf_tensor("hbuf", [P, dtt], F32))
        fxb = [es_D.enter_context(nc.sbuf_tensor("fx%d" % i, [P, dtt], F32))
               for i in range(4)]
        osb = [es_D.enter_context(nc.sbuf_tensor("osd%d" % i, [P, dtt], F32))
               for i in range(2)]

        tt_d_base = C["tt"]
        tt_d = {}
        for tb in range(ndtb):
            for ds in range(KD):
                tt_d[(tb, ds)] = tt_d_base + 2 * (tb * KD + ds) + 2

        vv_d_base = C["vv"]
        d_order = []
        for tb in range(ndtb):
            for c in range(4):
                d_order.append(("fix", tb, c))
            for j in range(KD):
                d_order.append(("evac", tb, j))
                if j + 4 < KD:
                    d_order.append(("fix", tb, j + 4))
        fix_done = {}
        ev_done = {}
        for pos, (kind, tb, idx) in enumerate(d_order):
            if kind == "fix":
                fix_done[(tb, idx)] = vv_d_base + pos + 1
            else:
                ev_done[(tb, idx)] = vv_d_base + pos + 1

        NU_end_tt = tt_u_base + NU
        wdsem = ["wd0", "wd1"]
        xpsem = ["xp0", "xp1"]
        sosem = ["so0", "so1"]

        @block.gpsimd
        def _(gpsimd):
            gpsimd.wait_ge(S["tt"], NU_end_tt)
            gpsimd.wait_ge(S["vv"], vv_u_end)
            gpsimd.wait_ge(S["ss"], ss_u_end)
            gpsimd.wait_ge(S["wsc"], C["wsc"])
            for tb in range(ndtb):
                for ds in range(KD):
                    for hv in range(2):
                        i2 = 2 * (tb * KD + ds) + hv
                        if i2 >= 2:
                            gpsimd.wait_ge(S["tt"], tt_d_base + i2 - 1)
                        gpsimd.dma_start(
                            out=wdp[hv][:],
                            in_=wd_bf[ds][:, hv * HH * P:(hv + 1) * HH * P],
                        ).then_inc(S[wdsem[hv]], 16)
                        C.inc(wdsem[hv], 16)

        @block.sync
        def _(sync):
            sync.wait_ge(S["tt"], NU_end_tt)
            sync.wait_ge(S["vv"], vv_u_end)
            for n in sgsem:
                sync.wait_ge(S[n], sg_totals[n])
            for n in spsem:
                sync.wait_ge(S[n], sp_totals[n])
            for tb in range(ndtb):
                if tb >= 1:
                    sync.wait_ge(S["tt"], tt_d[(tb - 1, KD - 1)])
                for kh in range(HS):
                    sync.dma_start(
                        out=gin[:, kh * dtt:(kh + 1) * dtt],
                        in_=gated_sc[kh * P:(kh + 1) * P,
                                     tb * dtt:(tb + 1) * dtt],
                    ).then_inc(S["gi"], 16)
                    C.inc("gi", 16)
                for c in range(KD):
                    i = tb * KD + c
                    if i >= 2:
                        pv = divmod(i - 2, KD)
                        sync.wait_ge(S["vv"], fix_done[pv])
                    sync.dma_start(
                        out=xdb[i % 2][:],
                        in_=xT[c * P:(c + 1) * P, tb * dtt:(tb + 1) * dtt],
                    ).then_inc(S[xpsem[i % 2]], 16)
                    C.inc(xpsem[i % 2], 16)
                    sync.dma_start(
                        out=pcdb[i % 2][:],
                        in_=pc_sc[c * P:(c + 1) * P, tb * dtt:(tb + 1) * dtt],
                    ).then_inc(S[xpsem[i % 2]], 16)
                    C.inc(xpsem[i % 2], 16)

        @block.tensor
        def _(tensor):
            for tb in range(ndtb):
                tensor.wait_ge(S["gi"], 16 * HS * (tb + 1))
                for ds in range(KD):
                    i = tb * KD + ds
                    if i >= 2:
                        tensor.wait_ge(S["vv"],
                                       ev_done[((i - 2) // KD, (i - 2) % KD)])
                    elif i == 0:
                        tensor.wait_ge(S["vv"], vv_u_end)
                    mm = None
                    for hv in range(2):
                        tensor.wait_ge(S[wdsem[hv]], 16 * (tb * KD + ds + 1))
                        for khl in range(HH):
                            kh = hv * HH + khl
                            mm = tensor.matmul(
                                banks[i % 2][:, 0:dtt],
                                wdp[hv][:, khl * P:(khl + 1) * P],
                                gin[:, kh * dtt:(kh + 1) * dtt],
                                start=(kh == 0), stop=(kh == HS - 1))
                        mm.then_inc(S["tt"], 1)
                        C.inc("tt")
                    assert C["tt"] == tt_d[(tb, ds)], (C["tt"], tt_d[(tb, ds)])

        @block.vector
        def _(vector):
            vector.wait_ge(S["vv"], m_dlr)
            for kind, tb, idx in d_order:
                if kind == "fix":
                    c = idx
                    i = tb * KD + c
                    vector.wait_ge(S[xpsem[i % 2]], 32 * (i // 2 + 1))
                    if tb >= 1:
                        vector.wait_ge(S["vv"], fix_done[(tb - 1, c)])
                    if i >= 4:
                        vector.wait_ge(S["vv"],
                                       ev_done[((i - 4) // KD, (i - 4) % KD)])
                    fb = fxb[i % 4]
                    vector.scalar_tensor_tensor(
                        out=fb[:], in0=ug_ap(c, tb * dtt, dtt),
                        scalar=deff_c[:, c:c + 1], in1=xdb[i % 2][:],
                        op0=ALU.mult, op1=ALU.add)
                    vmark(vector.tensor_tensor_scan(
                        out=hbuf[:],
                        data0=aa_c[:, c:c + 1].broadcast_to([P, dtt]),
                        data1=zeros_tt[:, 0:dtt],
                        initial=hcar[:, c:c + 1],
                        op0=ALU.mult, op1=ALU.add))
                    vwait(vector)
                    vector.tensor_copy(out=hcar[:, c:c + 1],
                                       in_=hbuf[:, dtt - 1:dtt])
                    vmark(vector.scalar_tensor_tensor(
                        out=fb[:], in0=hbuf[:], scalar=cb_c[:, c:c + 1],
                        in1=fb[:], op0=ALU.mult, op1=ALU.add))
                    vwait(vector)
                    vector.tensor_add(out=fb[:], in0=fb[:],
                                      in1=pcdb[i % 2][:]).then_inc(S["vv"], 1)
                    assert C.inc("vv") == fix_done[(tb, c)]
                else:
                    ds = idx
                    i = tb * KD + ds
                    vector.wait_ge(S["tt"], tt_d[(tb, ds)])
                    if i >= 2:
                        vector.wait_ge(S[sosem[i % 2]], 16 * (i // 2))
                    vector.tensor_add(out=osb[i % 2][:],
                                      in0=banks[i % 2][:, 0:dtt],
                                      in1=fxb[i % 4][:]).then_inc(S["vv"], 1)
                    assert C.inc("vv") == ev_done[(tb, ds)]

        @block.scalar
        def _(scalar):
            for tb in range(ndtb):
                for ds in range(KD):
                    i = tb * KD + ds
                    scalar.wait_ge(S["vv"], ev_done[(tb, ds)])
                    scalar.dma_start(
                        out=outT[ds * P:(ds + 1) * P, tb * dtt:(tb + 1) * dtt],
                        in_=osb[i % 2][:]).then_inc(S[sosem[i % 2]], 16)
                    C.inc(sosem[i % 2], 16)

        @block.scalar
        def _(scalar):
            scalar.wait_ge(S["so0"], C["so0"])
            scalar.wait_ge(S["so1"], C["so1"])

        nc.leave_named_scope("ph_D", _sid_D, False)
        es_D.close()

    return nc


def _host_layouts(inputs, tcore):
    u = np.asarray(inputs["L0_attn_u"])
    xTfull = np.ascontiguousarray(u[0].T)
    gwb = np.ascontiguousarray(np.asarray(inputs["L0_attn_gate_w"])
                               .reshape(KD, P, KD, P).transpose(0, 2, 3, 1))
    wgb = np.ascontiguousarray(np.asarray(inputs["L0_ffn_w_gate"])
                               .reshape(HS, P, KD, P).transpose(0, 2, 3, 1))
    wub = np.ascontiguousarray(np.asarray(inputs["L0_ffn_w_up"])
                               .reshape(HS, P, KD, P).transpose(0, 2, 3, 1))
    wdb = np.ascontiguousarray(np.asarray(inputs["L0_ffn_w_down"])
                               .reshape(KD, P, HS, P).transpose(0, 2, 3, 1))
    col = lambda v: np.ascontiguousarray(np.asarray(v).reshape(KD, P).T)
    base = dict(gw=gwb, wg=wgb, wu=wub, wdr=wdb,
                avec=col(inputs["L0_attn_a"]), bvec=col(inputs["L0_attn_b"]),
                cvec=col(inputs["L0_attn_c"]), dvec=col(inputs["L0_attn_d"]),
                gbvec=col(inputs["L0_attn_gate_b"]),
                ulr=np.ascontiguousarray(
                    np.asarray(inputs["L0_attn_d_lr_u"]).reshape(KD, P, 4)),
                vlr=np.ascontiguousarray(np.asarray(inputs["L0_attn_d_lr_v"])))
    in_maps = []
    for i in range(NCORES):
        m = dict(base)
        m["xT"] = np.ascontiguousarray(xTfull[:, i * tcore:(i + 1) * tcore])
        mask = np.zeros((P, KD, NCORES), dtype=np.float32)
        mask[:, :, :i] = 1.0
        m["maskin"] = mask.reshape(P, KD * NCORES)
        in_maps.append(m)
    return in_maps


def kernel(**inputs):
    u = np.asarray(inputs["L0_attn_u"])
    T = u.shape[1]
    tcore = T // NCORES
    tt = min(512, tcore)
    in_maps = _host_layouts(inputs, tcore)
    nc = build_nc(tcore, tt, T)
    res = run_bass_kernel_spmd(nc, in_maps, core_ids=list(range(NCORES)))
    outT = np.concatenate([r["outT"] for r in res.results], axis=1)
    return np.ascontiguousarray(outT.T).reshape(1, T, D).astype(np.float32)


# version 08
# speedup vs baseline: 1.0524x; 1.0279x over previous
import numpy as np
import concourse.bass as bass
import concourse.mybir as mybir
from concourse.bass_utils import run_bass_kernel_spmd
from contextlib import ExitStack

F32 = mybir.dt.float32
BF16 = mybir.dt.bfloat16
AF = mybir.ActivationFunctionType
ALU = mybir.AluOpType
AX = mybir.AxisListType

P = 128
D = 2048
H = 8192
KD = D // P
HS = H // P
NCORES = 8

SEM_NAMES = ["tt", "vv", "ss", "cc",
             "xc", "xl0", "xl1", "gw0", "gw1", "wu0", "wu1", "ca", "giA", "giB",
             "xp0", "xp1", "wd0", "wd1",
             "sp0", "sp1", "sp2", "sp3", "sp4", "sp5", "sp6", "sp7",
             "sc", "sg0", "sg1", "sg2", "sg3", "so0", "so1",
             "wsc", "dv"]


class Cnt:
    def __init__(self, names):
        self.v = {n: 0 for n in names}

    def inc(self, n, k=1):
        self.v[n] += k
        return self.v[n]

    def __getitem__(self, n):
        return self.v[n]


def build_nc(tcore, tt, t_total):
    ntb = tcore // tt
    assert tcore % tt == 0 and tt <= 512
    assert ntb <= 4
    ltt = min(512, tt)
    lnt = tcore // ltt

    nc = bass.Bass("TRN2", target_bir_lowering=False, debug=False)

    xT = nc.dram_tensor("xT", [D, tcore], F32, kind="ExternalInput").ap()
    gw = nc.dram_tensor("gw", [KD, KD, P, P], F32, kind="ExternalInput").ap()
    wg = nc.dram_tensor("wg", [HS, KD, P, P], F32, kind="ExternalInput").ap()
    wu = nc.dram_tensor("wu", [HS, KD, P, P], F32, kind="ExternalInput").ap()
    wdr = nc.dram_tensor("wdr", [KD, HS, P, P], F32, kind="ExternalInput").ap()
    avec = nc.dram_tensor("avec", [P, KD], F32, kind="ExternalInput").ap()
    bvec = nc.dram_tensor("bvec", [P, KD], F32, kind="ExternalInput").ap()
    cvec = nc.dram_tensor("cvec", [P, KD], F32, kind="ExternalInput").ap()
    dvec = nc.dram_tensor("dvec", [P, KD], F32, kind="ExternalInput").ap()
    gbvec = nc.dram_tensor("gbvec", [P, KD], F32, kind="ExternalInput").ap()
    ulr = nc.dram_tensor("ulr", [KD, P, 4], F32, kind="ExternalInput").ap()
    vlr = nc.dram_tensor("vlr", [4, D], F32, kind="ExternalInput").ap()
    maskin = nc.dram_tensor("maskin", [P, KD * NCORES], F32, kind="ExternalInput").ap()
    outT = nc.dram_tensor("outT", [D, tcore], F32, kind="ExternalOutput").ap()

    gated_sc = nc.dram_tensor("gated_sc", [H, tcore], BF16).ap()
    wd_bf = nc.dram_tensor("wd_bf", [KD, P, HS * P], BF16).ap()
    pc_sc = nc.dram_tensor("pc_sc", [D, tcore], BF16).ap()
    cc_in = nc.dram_tensor("cc_in", [P, 2 * KD], F32).ap()
    cc_out = nc.dram_tensor("cc_out", [NCORES * P, 2 * KD], F32).ap()

    C = Cnt(SEM_NAMES)

    with ExitStack() as ctx:
        S = {n: ctx.enter_context(nc.semaphore(n)) for n in SEM_NAMES}

        sb = lambda name, shape, dt: ctx.enter_context(nc.sbuf_tensor(name, shape, dt))
        dvn = [0]

        def vmark(inst):
            inst.then_inc(S["dv"], 1)
            dvn[0] += 1
            return dvn[0]

        def vwait(vector):
            vector.wait_ge(S["dv"], dvn[0])
        a_c = sb("a_c", [P, KD], F32)
        b_c = sb("b_c", [P, KD], F32)
        c_c = sb("c_c", [P, KD], F32)
        d_c = sb("d_c", [P, KD], F32)
        gb_c = sb("gb_c", [P, KD], F32)
        aa_c = sb("aa_c", [P, KD], F32)
        cb_c = sb("cb_c", [P, KD], F32)
        qm1 = sb("qm1", [P, KD], F32)
        dlr_c = sb("dlr_c", [P, KD], F32)
        deff_c = sb("deff_c", [P, KD], F32)
        carry = sb("carry", [P, KD], F32)
        hcar = sb("hcar", [P, KD], F32)
        Cc0 = sb("Cc0", [P, KD], F32)
        Cc1 = sb("Cc1", [P, KD], F32)
        usum_sl = sb("usum_sl", [P, KD * ntb], F32)
        pack_sb = sb("pack_sb", [P, 2 * KD], F32)
        mask_rep = sb("mask_rep", [P, KD * NCORES], F32)
        qeff_all = sb("qeff_all", [P, KD * NCORES], F32)
        mlt_all = sb("mlt_all", [P, KD * NCORES], F32)
        U_sb = sb("U_sb", [P, KD * 4], F32)
        V_sb = sb("V_sb", [4, D], F32)
        P4_sb = sb("P4_sb", [4, 1], F32)
        cc_all = sb("cc_all", [P, NCORES * 2 * KD], F32)
        ones_col = sb("ones_col", [P, 1], F32)
        ones_row = sb("ones_row", [1, P], F32)
        zeros_tt = sb("zeros_tt", [P, 512], F32)

        s_buf = sb("s_buf", [P, tt], F32)
        t2_buf = sb("t2_buf", [P, tt], F32)
        pcst = [sb("pcst%d" % i, [P, tt], BF16) for i in range(8)]

        es_xn = ExitStack()
        xn_bf = es_xn.enter_context(
            nc.sbuf_tensor("xn_bf", [P, KD * tcore], BF16, side="right"))
        ug_holder = {}

        banks = [ctx.enter_context(nc.psum_tensor("pb%d" % i, [P, 512], F32))
                 for i in range(8)]

        block = ctx.enter_context(nc.Block())

        def xn_ap(c, t0, n):
            return xn_bf[:, c * tcore + t0: c * tcore + t0 + n]

        def ug_ap(c, t0, n):
            ug_bf = ug_holder["t"]
            return ug_bf[:, c * tcore + t0: c * tcore + t0 + n]

        _sid_setup = nc.enter_named_scope("ph_setup", False)[0]
        @block.sync
        def _(sync):
            for src, dst in [(avec, a_c), (bvec, b_c), (cvec, c_c), (dvec, d_c),
                             (gbvec, gb_c), (maskin, mask_rep), (vlr, V_sb)]:
                sync.dma_start(out=dst[:], in_=src[:]).then_inc(S["xc"], 16)
                C.inc("xc", 16)
            sync.dma_start(out=U_sb[:].rearrange("p (c r) -> p c r", c=KD),
                           in_=ulr[:].transpose([1, 0, 2])).then_inc(S["xc"], 16)
            C.inc("xc", 16)
        m_consts = C["xc"]

        @block.scalar
        def _(scalar):
            scalar.wait_ge(S["xc"], m_consts)
            scalar.activation(out=aa_c[:], in_=a_c[:],
                              func=AF.Tanh).then_inc(S["ss"], 1)
            C.inc("ss")
        m_aa = C["ss"]

        @block.vector
        def _(vector):
            vector.memset(ones_col[:], 1.0)
            vector.memset(ones_row[:], 1.0)
            vector.memset(zeros_tt[:], 0.0)
            vector.memset(carry[:], 0.0)
            vector.memset(usum_sl[:], 0.0)
            vector.wait_ge(S["xc"], m_consts)
            vector.wait_ge(S["ss"], m_aa)
            vector.tensor_mul(out=cb_c[:], in0=c_c[:], in1=b_c[:])
            vmark(vector.tensor_mul(out=qm1[:], in0=aa_c[:], in1=aa_c[:]))
            nsq = int(round(np.log2(tcore)))
            assert 2 ** nsq == tcore
            for _i in range(nsq - 1):
                vwait(vector)
                vmark(vector.tensor_mul(out=qm1[:], in0=qm1[:], in1=qm1[:]))
            vwait(vector)
            vmark(vector.tensor_scalar_add(out=qm1[:], in0=qm1[:], scalar1=-1.0))
            vwait(vector)
            vector.engine_nop().then_inc(S["vv"], 1)
            C.inc("vv")
        m_setup_v = C["vv"]

        nc.leave_named_scope("ph_setup", _sid_setup, False)
        _sid_L = nc.enter_named_scope("ph_L", False)[0]
        es_L = ExitStack()
        x_sbs = [es_L.enter_context(
            nc.sbuf_tensor("x_sb%d" % i, [P, KD * ltt], F32, side="right"))
            for i in range(2)]
        sq_sb = es_L.enter_context(
            nc.sbuf_tensor("sq_sb", [P, KD * ltt], F32, side="right"))
        t1x = [es_L.enter_context(
            nc.sbuf_tensor("t1x%d" % i, [P, ltt], F32, side="right"))
            for i in range(2)]
        srows = es_L.enter_context(
            nc.sbuf_tensor("srows", [1, 7 * ltt], F32, side="right"))
        mrow = srows[:, 0 * ltt:1 * ltt]
        e2row = srows[:, 1 * ltt:2 * ltt]
        vrow = srows[:, 2 * ltt:3 * ltt]
        r0row = srows[:, 3 * ltt:4 * ltt]
        t1row = srows[:, 4 * ltt:5 * ltt]
        rstdrow = srows[:, 5 * ltt:6 * ltt]
        nmrow = srows[:, 6 * ltt:7 * ltt]

        xn_done = [None] * lnt
        stats_done = [None] * lnt
        m_invs = [None] * lnt

        for k in range(lnt):
            x_sb = x_sbs[k % 2]
            xls = "xl%d" % (k % 2)

            @block.sync
            def _(sync):
                if k >= 2:
                    sync.wait_ge(S["vv"], xn_done[k - 2])
                for c in range(KD):
                    sync.dma_start(
                        out=x_sb[:, c * ltt:(c + 1) * ltt],
                        in_=xT[c * P:(c + 1) * P, k * ltt:(k + 1) * ltt],
                    ).then_inc(S[xls], 16)
                    C.inc(xls, 16)
            m_xl = C[xls]

            ss_base = C["ss"]

            @block.scalar
            def _(scalar):
                if k >= 1:
                    scalar.wait_ge(S["tt"], stats_done[k - 1])
                scalar.wait_ge(S[xls], m_xl)
                for c in range(KD):
                    scalar.activation(
                        out=sq_sb[:, c * ltt:(c + 1) * ltt],
                        in_=x_sb[:, c * ltt:(c + 1) * ltt],
                        func=AF.Square).then_inc(S["ss"], 1)
                    C.inc("ss")

            @block.tensor
            def _(tensor):
                if k >= 1:
                    tensor.wait_ge(S["vv"], m_invs[k - 1])
                else:
                    tensor.wait_ge(S["vv"], m_setup_v)
                mm = None
                for c in range(KD):
                    tensor.wait_ge(S["ss"], ss_base + c + 1)
                    tensor.matmul(banks[0][0:1, 0:ltt], ones_col[:],
                                  x_sb[:, c * ltt:(c + 1) * ltt],
                                  start=(c == 0), stop=(c == KD - 1))
                    mm = tensor.matmul(banks[1][0:1, 0:ltt], ones_col[:],
                                       sq_sb[:, c * ltt:(c + 1) * ltt],
                                       start=(c == 0), stop=(c == KD - 1))
                mm.then_inc(S["tt"], 1)
                stats_done[k] = C.inc("tt")

            @block.vector
            def _(vector):
                vector.wait_ge(S["tt"], stats_done[k])
                inv = 1.0 / D
                vector.tensor_scalar_mul(out=mrow, in0=banks[0][0:1, 0:ltt],
                                         scalar1=inv)
                vmark(vector.tensor_scalar_mul(out=e2row, in0=banks[1][0:1, 0:ltt],
                                               scalar1=inv))
                vwait(vector)
                vmark(vector.tensor_mul(out=t1row, in0=mrow, in1=mrow))
                vwait(vector)
                vmark(vector.tensor_sub(out=vrow, in0=e2row, in1=t1row))
                vwait(vector)
                vmark(vector.tensor_scalar_add(out=vrow, in0=vrow, scalar1=1e-5))
                vwait(vector)
                vector.reciprocal(out=r0row, in_=vrow).then_inc(S["vv"], 1)
                m_invs[k] = C.inc("vv")
            m_inv = C["vv"]

            @block.scalar
            def _(scalar):
                scalar.wait_ge(S["vv"], m_inv)
                scalar.activation(out=r0row, in_=r0row,
                                  func=AF.Sqrt).then_inc(S["ss"], 1)
                C.inc("ss")
            m_sqrt = C["ss"]

            @block.vector
            def _(vector):
                vector.wait_ge(S["ss"], m_sqrt)
                vmark(vector.tensor_mul(out=t1row, in0=r0row, in1=r0row))
                vwait(vector)
                vmark(vector.tensor_mul(out=t1row, in0=t1row, in1=vrow))
                vwait(vector)
                vmark(vector.tensor_scalar(out=t1row, in0=t1row, scalar1=-0.5,
                                           scalar2=1.5, op0=ALU.mult, op1=ALU.add))
                vwait(vector)
                vmark(vector.tensor_mul(out=rstdrow, in0=r0row, in1=t1row))
                vwait(vector)
                vmark(vector.tensor_mul(out=nmrow, in0=mrow, in1=rstdrow))
                vwait(vector)
                vector.tensor_scalar_mul(out=nmrow, in0=nmrow,
                                         scalar1=-1.0).then_inc(S["vv"], 1)
                C.inc("vv")
            m_rows = C["vv"]

            @block.tensor
            def _(tensor):
                tensor.wait_ge(S["vv"], m_rows)
                tensor.matmul(banks[2][:, 0:ltt], ones_row[:], rstdrow,
                              start=True, stop=True)
                tensor.matmul(banks[3][:, 0:ltt], ones_row[:], nmrow,
                              start=True, stop=True).then_inc(S["tt"], 1)
                C.inc("tt")
            m_bc = C["tt"]

            @block.vector
            def _(vector):
                vector.wait_ge(S["tt"], m_bc)
                last = None
                for c in range(KD):
                    if c >= 2:
                        vwait(vector)
                    vmark(vector.tensor_mul(out=t1x[c % 2][:],
                                            in0=x_sb[:, c * ltt:(c + 1) * ltt],
                                            in1=banks[2][:, 0:ltt]))
                    vwait(vector)
                    last = vector.tensor_add(out=xn_ap(c, k * ltt, ltt),
                                             in0=t1x[c % 2][:],
                                             in1=banks[3][:, 0:ltt])
                    if c < KD - 1:
                        vmark(last)
                last.then_inc(S["vv"], 1)
                xn_done[k] = C.inc("vv")

        nc.leave_named_scope("ph_L", _sid_L, False)
        _sid_G = nc.enter_named_scope("ph_G", False)[0]
        es_L.close()
        ug_holder["t"] = ctx.enter_context(
            nc.sbuf_tensor("ug_bf", [P, KD * tcore], BF16))
        es_G = ExitStack()
        gw_pan = [es_G.enter_context(nc.sbuf_tensor("gwp%d" % i, [P, KD * P], BF16))
                  for i in range(2)]
        gs_bufs = [es_G.enter_context(nc.sbuf_tensor("gs%d" % i, [P, tt], F32))
                   for i in range(ntb)]

        tt_g_base = C["tt"]
        ss_g_base = C["ss"]
        vv_g_base = C["vv"]
        gwsem = ["gw0", "gw1"]

        @block.gpsimd
        def _(gpsimd):
            gpsimd.wait_ge(S["vv"], xn_done[lnt - 1])
            for es in range(KD):
                if es >= 2:
                    gpsimd.wait_ge(S["tt"], tt_g_base + es - 1)
                gpsimd.dma_start(
                    out=gw_pan[es % 2][:].rearrange("p (c j) -> p c j", c=KD),
                    in_=gw[es].transpose([1, 0, 2])).then_inc(S[gwsem[es % 2]], 16)
                C.inc(gwsem[es % 2], 16)

        @block.tensor
        def _(tensor):
            tensor.wait_ge(S["vv"], xn_done[lnt - 1])
            for es in range(KD):
                tensor.wait_ge(S[gwsem[es % 2]], 16 * (es // 2 + 1))
                if es >= 2:
                    tensor.wait_ge(S["ss"], ss_g_base + (es - 1) * ntb)
                grp = (es % 2) * 4
                mm = None
                for kd in range(KD):
                    for tb in range(ntb):
                        mm = tensor.matmul(
                            banks[grp + tb][:, 0:tt],
                            gw_pan[es % 2][:, kd * P:(kd + 1) * P],
                            xn_ap(kd, tb * tt, tt),
                            start=(kd == 0), stop=(kd == KD - 1))
                mm.then_inc(S["tt"], 1)
                C.inc("tt")

        @block.scalar
        def _(scalar):
            for es in range(KD):
                scalar.wait_ge(S["tt"], tt_g_base + es + 1)
                if es >= 1:
                    scalar.wait_ge(S["vv"], vv_g_base + es)
                grp = (es % 2) * 4
                for tb in range(ntb):
                    scalar.activation(
                        out=gs_bufs[tb][:], in_=banks[grp + tb][:, 0:tt],
                        func=AF.Sigmoid, bias=gb_c[:, es:es + 1],
                    ).then_inc(S["ss"], 1)
                    C.inc("ss")

        @block.vector
        def _(vector):
            for es in range(KD):
                for tb in range(ntb):
                    vector.wait_ge(S["ss"], ss_g_base + es * ntb + tb + 1)
                    inst = vector.scalar_tensor_tensor(
                        out=ug_ap(es, tb * tt, tt),
                        in0=gs_bufs[tb][:], scalar=1.0,
                        in1=xn_ap(es, tb * tt, tt),
                        op0=ALU.mult, op1=ALU.mult,
                        accum_out=usum_sl[:, es * ntb + tb:es * ntb + tb + 1])
                inst.then_inc(S["vv"], 1)
                C.inc("vv")
        vv_g_end = C["vv"]
        ss_g_end = C["ss"]

        nc.leave_named_scope("ph_G", _sid_G, False)
        _sid_S = nc.enter_named_scope("ph_S", False)[0]
        spsem = ["sp%d" % i for i in range(8)]
        m_pcs = {}
        m_pack = [None]

        def make_s_chunk_emitter():
            sbufs = [s_buf, t2_buf]
            state = {"idx": 0}

            def emit_one(vector):
                idx = state["idx"]
                nchunks = ntb * KD
                if idx > nchunks:
                    return
                if idx == nchunks:
                    vector.wait_ge(S["dv"], m_pcs[(ntb - 1, KD - 1)])
                    for c in range(KD):
                        vector.reduce_sum(out=pack_sb[:, c:c + 1],
                                          in_=usum_sl[:, c * ntb:(c + 1) * ntb],
                                          axis=AX.X)
                    m_pack[0] = vmark(vector.tensor_copy(
                        out=pack_sb[:, KD:2 * KD], in_=carry[:]))
                    state["idx"] += 1
                    return
                k, c = divmod(idx, KD)
                if idx == 0:
                    vector.wait_ge(S["vv"], vv_g_end)
                if idx >= 8:
                    vector.wait_ge(S[spsem[idx % 8]], 16 * (idx // 8))
                if idx >= 2:
                    vector.wait_ge(S["dv"], m_pcs[((idx - 2) // KD,
                                                  (idx - 2) % KD)])
                vmark(vector.tensor_tensor_scan(
                    out=sbufs[idx % 2][:],
                    data0=aa_c[:, c:c + 1].broadcast_to([P, tt]),
                    data1=ug_ap(c, k * tt, tt),
                    initial=(0.0 if k == 0 else carry[:, c:c + 1]),
                    op0=ALU.mult, op1=ALU.add))
                vwait(vector)
                vector.tensor_copy(out=carry[:, c:c + 1],
                                   in_=sbufs[idx % 2][:, tt - 1:tt])
                m_pcs[(k, c)] = vmark(vector.tensor_scalar_mul(
                    out=pcst[idx % 8][:],
                    in0=sbufs[idx % 2][:],
                    scalar1=cb_c[:, c:c + 1]))
                state["idx"] += 1

            return emit_one, state

        def make_s_store_emitter():
            state = {"idx": 0}

            def emit_one(sync):
                idx = state["idx"]
                nchunks = ntb * KD
                if idx > nchunks:
                    return
                if idx == nchunks:
                    sync.wait_ge(S["dv"], m_pack[0])
                    sync.dma_start(out=cc_in[:],
                                   in_=pack_sb[:]).then_inc(S["sc"], 16)
                    C.inc("sc", 16)
                    state["idx"] += 1
                    return
                k, c = divmod(idx, KD)
                sync.wait_ge(S["dv"], m_pcs[(k, c)])
                sync.dma_start(
                    out=pc_sc[c * P:(c + 1) * P, k * tt:(k + 1) * tt],
                    in_=pcst[idx % 8][:]).then_inc(S[spsem[idx % 8]], 16)
                C.inc(spsem[idx % 8], 16)
                state["idx"] += 1

            return emit_one, state

        nc.leave_named_scope("ph_S", _sid_S, False)
        _sid_U = nc.enter_named_scope("ph_U", False)[0]
        es_G.close()
        es_U = ExitStack()
        wgp = [es_U.enter_context(nc.sbuf_tensor("wgpu%d" % i, [P, KD * P], BF16))
               for i in range(2)]
        wup = [es_U.enter_context(nc.sbuf_tensor("wupu%d" % i, [P, KD * P], BF16))
               for i in range(2)]
        silub = [es_U.enter_context(nc.sbuf_tensor("silu%d" % i, [P, 512], F32))
                 for i in range(2)]
        gstage = [es_U.enter_context(nc.sbuf_tensor("gst%d" % i, [P, 512], BF16))
                  for i in range(4)]

        sub_t = min(512, tcore)
        nsb = tcore // sub_t
        g = min(2, nsb)
        nst = nsb // g
        NU = HS * nst

        tt_u_base = C["tt"]
        ss_u_base = C["ss"]
        vv_u_base = C["vv"]
        wusem = ["wu0", "wu1"]
        sgsem = ["sg0", "sg1", "sg2", "sg3"]

        def emit_u_panels(gpsimd, hs_range):
            for hs in hs_range:
                if hs >= 2 and hs % 4 == 2:
                    ds = (hs - 2) // 4
                    gpsimd.dma_start(
                        out=wd_bf[ds].rearrange("p (k j) -> p k j", k=HS),
                        in_=wdr[ds].transpose([1, 0, 2]),
                    ).then_inc(S["wsc"], 16)
                    C.inc("wsc", 16)
                if hs < 2:
                    gpsimd.wait_ge(S["tt"], tt_g_base + KD)
                    gpsimd.wait_ge(S["vv"], vv_g_end)
                if hs >= 2:
                    gpsimd.wait_ge(S["tt"], tt_u_base + (hs - 1) * nst)
                gpsimd.dma_start(
                    out=wgp[hs % 2][:].rearrange("p (c j) -> p c j", c=KD),
                    in_=wg[hs].transpose([1, 0, 2])).then_inc(S[wusem[hs % 2]], 16)
                C.inc(wusem[hs % 2], 16)
                gpsimd.dma_start(
                    out=wup[hs % 2][:].rearrange("p (c j) -> p c j", c=KD),
                    in_=wu[hs].transpose([1, 0, 2])).then_inc(S[wusem[hs % 2]], 16)
                C.inc(wusem[hs % 2], 16)

        @block.gpsimd
        def _(gpsimd):
            emit_u_panels(gpsimd, range(0, 2))

        @block.gpsimd
        def _(gpsimd):
            emit_u_panels(gpsimd, range(2, 32))

        @block.gpsimd
        def _(gpsimd):
            gpsimd.wait_ge(S["sc"], 16)
            gpsimd.collective_compute(
                "AllGather", ALU.bypass,
                replica_groups=[list(range(NCORES))],
                ins=[cc_in[:]], outs=[cc_out[:]],
            ).then_inc(S["cc"], 1)
            C.inc("cc")

        @block.gpsimd
        def _(gpsimd):
            gpsimd.wait_ge(S["cc"], 1)
            gpsimd.dma_start(
                out=cc_all[:].rearrange("p (j s) -> p j s", j=NCORES),
                in_=cc_out[:].rearrange("(j p) s -> p j s", p=P),
            ).then_inc(S["ca"], 16)
            C.inc("ca", 16)
        m_ccload = C["ca"]

        @block.gpsimd
        def _(gpsimd):
            emit_u_panels(gpsimd, range(32, HS))

        @block.tensor
        def _(tensor):
            for hs in range(HS):
                tensor.wait_ge(S[wusem[hs % 2]], 32 * (hs // 2 + 1))
                for si in range(nst):
                    u = hs * nst + si
                    if u < 2:
                        gdone = KD - 2 + u
                        tensor.wait_ge(S["ss"], ss_g_base + (gdone + 1) * ntb)
                        tensor.wait_ge(S["vv"], vv_g_base + gdone + 1)
                    else:
                        tensor.wait_ge(S["ss"], ss_u_base + (u - 1) * g)
                        tensor.wait_ge(S["vv"], vv_u_base + (u - 1) * g)
                    grp = (u % 2) * 4
                    mm = None
                    for kd in range(KD):
                        for ib in range(g):
                            ts0 = (si * g + ib) * sub_t
                            tensor.matmul(
                                banks[grp + ib][:, 0:sub_t],
                                wgp[hs % 2][:, kd * P:(kd + 1) * P],
                                xn_ap(kd, ts0, sub_t),
                                start=(kd == 0), stop=(kd == KD - 1))
                        for ib in range(g):
                            ts0 = (si * g + ib) * sub_t
                            mm = tensor.matmul(
                                banks[grp + g + ib][:, 0:sub_t],
                                wup[hs % 2][:, kd * P:(kd + 1) * P],
                                xn_ap(kd, ts0, sub_t),
                                start=(kd == 0), stop=(kd == KD - 1))
                    mm.then_inc(S["tt"], 1)
                    C.inc("tt")

        def emit_u_silu(scalar, hs_range):
            for hs in hs_range:
                for si in range(nst):
                    u = hs * nst + si
                    scalar.wait_ge(S["tt"], tt_u_base + u + 1)
                    if u >= 1:
                        scalar.wait_ge(S["vv"], vv_u_base + u * g)
                    grp = (u % 2) * 4
                    for ib in range(g):
                        scalar.activation(
                            out=silub[ib][:, 0:sub_t],
                            in_=banks[grp + ib][:, 0:sub_t],
                            func=AF.Silu).then_inc(S["ss"], 1)
                        C.inc("ss")


        s_chunk, s_state = make_s_chunk_emitter()
        s_store, s_store_state = make_s_store_emitter()

        def emit_u_evac(vector, hs_range):
            for hs in hs_range:
                for si in range(nst):
                    u = hs * nst + si
                    grp = (u % 2) * 4
                    for ib in range(g):
                        gi = u * g + ib
                        vector.wait_ge(S["ss"], ss_u_base + gi + 1)
                        if gi >= 4:
                            vector.wait_ge(S[sgsem[gi % 4]], 16 * (gi // 4))
                        vector.tensor_mul(
                            out=gstage[gi % 4][:, 0:sub_t],
                            in0=silub[ib][:, 0:sub_t],
                            in1=banks[grp + g + ib][:, 0:sub_t],
                            ).then_inc(S["vv"], 1)
                        C.inc("vv")
                        s_chunk(vector)

        @block.vector
        def _(vector):
            emit_u_evac(vector, range(0, 32))

        @block.vector
        def _(vector):
            vector.wait_ge(S["ca"], m_ccload)
            stride = 2 * KD
            _ca = cc_all[:]
            usv = bass.AP(_ca.tensor, _ca.offset,
                          [_ca.ap[0], [1, KD], [stride, NCORES]])
            vector.reduce_sum(out=pack_sb[:, 0:KD], in_=usv, axis=AX.X)
            lv = bass.AP(_ca.tensor, _ca.offset + KD,
                         [_ca.ap[0], [1, KD], [stride, NCORES]])
            mrv = mask_rep[:].rearrange("p (c j) -> p c j", c=KD)
            vector.tensor_mul(out=mlt_all[:].rearrange("p (c j) -> p c j", c=KD),
                              in0=mrv, in1=lv)
            _qm = qm1[:]
            qmv = bass.AP(_qm.tensor, _qm.offset,
                          [_qm.ap[0], [1, KD], [0, NCORES]])
            vmark(vector.tensor_mul(
                out=qeff_all[:].rearrange("p (c j) -> p c j", c=KD),
                in0=mrv, in1=qmv))
            vwait(vector)
            vmark(vector.tensor_scalar_add(out=qeff_all[:], in0=qeff_all[:],
                                           scalar1=1.0))
            vmark(vector.memset(Cc0[:], 0.0))
            vwait(vector)
            bufs = [Cc0, Cc1]
            for j in range(NCORES):
                _qa, _ma = qeff_all[:], mlt_all[:]
                qv = bass.AP(_qa.tensor, _qa.offset + j,
                             [_qa.ap[0], [NCORES, KD]])
                mv = bass.AP(_ma.tensor, _ma.offset + j,
                             [_ma.ap[0], [NCORES, KD]])
                vmark(vector.tensor_mul(out=bufs[(j + 1) % 2][:],
                                        in0=bufs[j % 2][:], in1=qv))
                vwait(vector)
                vmark(vector.tensor_add(out=bufs[(j + 1) % 2][:],
                                        in0=bufs[(j + 1) % 2][:], in1=mv))
                vwait(vector)
            vmark(vector.tensor_copy(out=hcar[:], in_=bufs[NCORES % 2][:]))
        m_usg_dv = dvn[0]

        @block.vector
        def _(vector):
            emit_u_evac(vector, range(32, HS))

        @block.scalar
        def _(scalar):
            emit_u_silu(scalar, range(0, HS))

        @block.sync
        def _(sync):
            for hs in range(HS):
                for si in range(nst):
                    u = hs * nst + si
                    for ib in range(g):
                        gi = u * g + ib
                        sync.wait_ge(S["vv"], vv_u_base + gi + 1)
                        ts0 = (si * g + ib) * sub_t
                        sync.dma_start(
                            out=gated_sc[hs * P:(hs + 1) * P, ts0:ts0 + sub_t],
                            in_=gstage[gi % 4][:, 0:sub_t],
                        ).then_inc(S[sgsem[gi % 4]], 16)
                        C.inc(sgsem[gi % 4], 16)
                        if gi >= 1:
                            s_store(sync)
            while s_store_state["idx"] <= ntb * KD:
                s_store(sync)
        vv_u_end = C["vv"]
        ss_u_end = C["ss"]
        sg_totals = {n: C[n] for n in sgsem}
        sp_totals = {n: C[n] for n in spsem}

        nc.leave_named_scope("ph_U", _sid_U, False)
        _sid_CC = nc.enter_named_scope("ph_CC", False)[0]
        @block.tensor
        def _(tensor):
            tensor.wait_ge(S["dv"], m_usg_dv)
            tensor.wait_ge(S["ss"], ss_u_end)
            mm = None
            for c in range(KD):
                mm = tensor.matmul(banks[2][0:4, 0:1],
                                   U_sb[:, c * 4:(c + 1) * 4],
                                   pack_sb[:, c:c + 1],
                                   start=(c == 0), stop=(c == KD - 1))
            mm.then_inc(S["tt"], 1)
            C.inc("tt")
        m_p4 = C["tt"]

        @block.vector
        def _(vector):
            vector.wait_ge(S["tt"], m_p4)
            vector.tensor_copy(out=P4_sb[:],
                               in_=banks[2][0:4, 0:1]).then_inc(S["vv"], 1)
            C.inc("vv")
        m_p4v = C["vv"]

        @block.tensor
        def _(tensor):
            tensor.wait_ge(S["vv"], m_p4v)
            for c in range(KD):
                if c >= 2:
                    tensor.wait_ge(S["vv"], m_p4v + c - 1)
                tensor.matmul(banks[3 + (c % 2)][:, 0:1],
                              V_sb[0:4, c * P:(c + 1) * P], P4_sb[:],
                              start=True, stop=True).then_inc(S["tt"], 1)
                C.inc("tt")
        m_dlr_t = C["tt"]

        @block.vector
        def _(vector):
            sc = 1.0 / (float(t_total) * 4.0)
            for c in range(KD):
                vector.wait_ge(S["tt"], m_dlr_t - KD + c + 1)
                vector.tensor_scalar_mul(out=dlr_c[:, c:c + 1],
                                         in0=banks[3 + (c % 2)][:, 0:1],
                                         scalar1=sc).then_inc(S["vv"], 1)
                C.inc("vv")
            vector.wait_ge(S["vv"], C["vv"])
            vector.tensor_add(out=deff_c[:], in0=dlr_c[:],
                              in1=d_c[:]).then_inc(S["vv"], 1)
            C.inc("vv")
        m_dlr = C["vv"]

        es_U.close()
        es_xn.close()

        nc.leave_named_scope("ph_CC", _sid_CC, False)
        _sid_D = nc.enter_named_scope("ph_D", False)[0]
        es_D = ExitStack()
        dtt = min(512, tcore)
        ndtb = tcore // dtt
        HH = HS // 2
        wdp = [es_D.enter_context(
            nc.sbuf_tensor("wdp%d" % i, [P, HH * P], BF16)) for i in range(2)]
        ginA = es_D.enter_context(nc.sbuf_tensor("ginA", [P, (HS // 2) * dtt], BF16))
        ginB = es_D.enter_context(nc.sbuf_tensor("ginB", [P, (HS // 2) * dtt], BF16))
        xdb = [es_D.enter_context(nc.sbuf_tensor("xdbuf%d" % i, [P, dtt], F32))
               for i in range(2)]
        pcdb = [es_D.enter_context(nc.sbuf_tensor("pcd%d" % i, [P, dtt], BF16))
                for i in range(2)]
        hbuf = es_D.enter_context(nc.sbuf_tensor("hbuf", [P, dtt], F32))
        fxb = [es_D.enter_context(nc.sbuf_tensor("fx%d" % i, [P, dtt], F32))
               for i in range(4)]
        osb = [es_D.enter_context(nc.sbuf_tensor("osd%d" % i, [P, dtt], F32))
               for i in range(2)]

        tt_d_base = C["tt"]
        tt_d = {}
        for tb in range(ndtb):
            for ds in range(KD):
                tt_d[(tb, ds)] = tt_d_base + 2 * (tb * KD + ds) + 2

        vv_d_base = C["vv"]
        d_order = []
        for tb in range(ndtb):
            for c in range(4):
                d_order.append(("fix", tb, c))
            for j in range(KD):
                d_order.append(("evac", tb, j))
                if j + 4 < KD:
                    d_order.append(("fix", tb, j + 4))
        fix_done = {}
        ev_done = {}
        for pos, (kind, tb, idx) in enumerate(d_order):
            if kind == "fix":
                fix_done[(tb, idx)] = vv_d_base + pos + 1
            else:
                ev_done[(tb, idx)] = vv_d_base + pos + 1

        NU_end_tt = tt_u_base + NU
        wdsem = ["wd0", "wd1"]
        xpsem = ["xp0", "xp1"]
        sosem = ["so0", "so1"]

        @block.gpsimd
        def _(gpsimd):
            gpsimd.wait_ge(S["tt"], NU_end_tt)
            gpsimd.wait_ge(S["vv"], vv_u_end)
            gpsimd.wait_ge(S["ss"], ss_u_end)
            gpsimd.wait_ge(S["wsc"], C["wsc"])
            for tb in range(ndtb):
                for ds in range(KD):
                    for hv in range(2):
                        i2 = 2 * (tb * KD + ds) + hv
                        if i2 >= 2:
                            gpsimd.wait_ge(S["tt"], tt_d_base + i2 - 1)
                        gpsimd.dma_start(
                            out=wdp[hv][:],
                            in_=wd_bf[ds][:, hv * HH * P:(hv + 1) * HH * P],
                        ).then_inc(S[wdsem[hv]], 16)
                        C.inc(wdsem[hv], 16)

        @block.sync
        def _(sync):
            sync.wait_ge(S["tt"], NU_end_tt)
            sync.wait_ge(S["vv"], vv_u_end)
            for n in sgsem:
                sync.wait_ge(S[n], sg_totals[n])
            for n in spsem:
                sync.wait_ge(S[n], sp_totals[n])
            for tb in range(ndtb):
                if tb >= 1:
                    sync.wait_ge(S["tt"], tt_d[(tb - 1, KD - 1)] - 1)
                sync.dma_start(
                    out=ginA[:].rearrange("p (k t) -> p k t", k=HS // 2),
                    in_=gated_sc[0:(HS // 2) * P, tb * dtt:(tb + 1) * dtt]
                        .rearrange("(k p) t -> p k t", p=P),
                ).then_inc(S["giA"], 16)
                C.inc("giA", 16)
                if tb >= 1:
                    sync.wait_ge(S["tt"], tt_d[(tb - 1, KD - 1)])
                sync.dma_start(
                    out=ginB[:].rearrange("p (k t) -> p k t", k=HS // 2),
                    in_=gated_sc[(HS // 2) * P:HS * P, tb * dtt:(tb + 1) * dtt]
                        .rearrange("(k p) t -> p k t", p=P),
                ).then_inc(S["giB"], 16)
                C.inc("giB", 16)
                for c in range(KD):
                    i = tb * KD + c
                    if i >= 2:
                        pv = divmod(i - 2, KD)
                        sync.wait_ge(S["vv"], fix_done[pv])
                    sync.dma_start(
                        out=xdb[i % 2][:],
                        in_=xT[c * P:(c + 1) * P, tb * dtt:(tb + 1) * dtt],
                    ).then_inc(S[xpsem[i % 2]], 16)
                    C.inc(xpsem[i % 2], 16)
                    sync.dma_start(
                        out=pcdb[i % 2][:],
                        in_=pc_sc[c * P:(c + 1) * P, tb * dtt:(tb + 1) * dtt],
                    ).then_inc(S[xpsem[i % 2]], 16)
                    C.inc(xpsem[i % 2], 16)

        @block.tensor
        def _(tensor):
            gins = [ginA, ginB]
            gisem = ["giA", "giB"]
            for tb in range(ndtb):
                for ds in range(KD):
                    i = tb * KD + ds
                    if i >= 2:
                        tensor.wait_ge(S["vv"],
                                       ev_done[((i - 2) // KD, (i - 2) % KD)])
                    elif i == 0:
                        tensor.wait_ge(S["vv"], vv_u_end)
                    mm = None
                    for hv in range(2):
                        tensor.wait_ge(S[gisem[hv]], 16 * (tb + 1))
                        tensor.wait_ge(S[wdsem[hv]], 16 * (tb * KD + ds + 1))
                        for khl in range(HH):
                            kh = hv * HH + khl
                            mm = tensor.matmul(
                                banks[i % 2][:, 0:dtt],
                                wdp[hv][:, khl * P:(khl + 1) * P],
                                gins[hv][:, khl * dtt:(khl + 1) * dtt],
                                start=(kh == 0), stop=(kh == HS - 1))
                        mm.then_inc(S["tt"], 1)
                        C.inc("tt")
                    assert C["tt"] == tt_d[(tb, ds)], (C["tt"], tt_d[(tb, ds)])

        @block.vector
        def _(vector):
            vector.wait_ge(S["vv"], m_dlr)
            for kind, tb, idx in d_order:
                if kind == "fix":
                    c = idx
                    i = tb * KD + c
                    vector.wait_ge(S[xpsem[i % 2]], 32 * (i // 2 + 1))
                    if tb >= 1:
                        vector.wait_ge(S["vv"], fix_done[(tb - 1, c)])
                    if i >= 4:
                        vector.wait_ge(S["vv"],
                                       ev_done[((i - 4) // KD, (i - 4) % KD)])
                    fb = fxb[i % 4]
                    vector.scalar_tensor_tensor(
                        out=fb[:], in0=ug_ap(c, tb * dtt, dtt),
                        scalar=deff_c[:, c:c + 1], in1=xdb[i % 2][:],
                        op0=ALU.mult, op1=ALU.add)
                    vmark(vector.tensor_tensor_scan(
                        out=hbuf[:],
                        data0=aa_c[:, c:c + 1].broadcast_to([P, dtt]),
                        data1=zeros_tt[:, 0:dtt],
                        initial=hcar[:, c:c + 1],
                        op0=ALU.mult, op1=ALU.add))
                    vwait(vector)
                    vector.tensor_copy(out=hcar[:, c:c + 1],
                                       in_=hbuf[:, dtt - 1:dtt])
                    vmark(vector.scalar_tensor_tensor(
                        out=fb[:], in0=hbuf[:], scalar=cb_c[:, c:c + 1],
                        in1=fb[:], op0=ALU.mult, op1=ALU.add))
                    vwait(vector)
                    vector.tensor_add(out=fb[:], in0=fb[:],
                                      in1=pcdb[i % 2][:]).then_inc(S["vv"], 1)
                    assert C.inc("vv") == fix_done[(tb, c)]
                else:
                    ds = idx
                    i = tb * KD + ds
                    vector.wait_ge(S["tt"], tt_d[(tb, ds)])
                    if i >= 2:
                        vector.wait_ge(S[sosem[i % 2]], 16 * (i // 2))
                    vector.tensor_add(out=osb[i % 2][:],
                                      in0=banks[i % 2][:, 0:dtt],
                                      in1=fxb[i % 4][:]).then_inc(S["vv"], 1)
                    assert C.inc("vv") == ev_done[(tb, ds)]

        @block.scalar
        def _(scalar):
            for tb in range(ndtb):
                for ds in range(KD):
                    i = tb * KD + ds
                    scalar.wait_ge(S["vv"], ev_done[(tb, ds)])
                    scalar.dma_start(
                        out=outT[ds * P:(ds + 1) * P, tb * dtt:(tb + 1) * dtt],
                        in_=osb[i % 2][:]).then_inc(S[sosem[i % 2]], 16)
                    C.inc(sosem[i % 2], 16)

        @block.scalar
        def _(scalar):
            scalar.wait_ge(S["so0"], C["so0"])
            scalar.wait_ge(S["so1"], C["so1"])

        nc.leave_named_scope("ph_D", _sid_D, False)
        es_D.close()

    return nc


def _host_layouts(inputs, tcore):
    u = np.asarray(inputs["L0_attn_u"])
    xTfull = np.ascontiguousarray(u[0].T)
    gwb = np.ascontiguousarray(np.asarray(inputs["L0_attn_gate_w"])
                               .reshape(KD, P, KD, P).transpose(0, 2, 3, 1))
    wgb = np.ascontiguousarray(np.asarray(inputs["L0_ffn_w_gate"])
                               .reshape(HS, P, KD, P).transpose(0, 2, 3, 1))
    wub = np.ascontiguousarray(np.asarray(inputs["L0_ffn_w_up"])
                               .reshape(HS, P, KD, P).transpose(0, 2, 3, 1))
    wdb = np.ascontiguousarray(np.asarray(inputs["L0_ffn_w_down"])
                               .reshape(KD, P, HS, P).transpose(0, 2, 3, 1))
    col = lambda v: np.ascontiguousarray(np.asarray(v).reshape(KD, P).T)
    base = dict(gw=gwb, wg=wgb, wu=wub, wdr=wdb,
                avec=col(inputs["L0_attn_a"]), bvec=col(inputs["L0_attn_b"]),
                cvec=col(inputs["L0_attn_c"]), dvec=col(inputs["L0_attn_d"]),
                gbvec=col(inputs["L0_attn_gate_b"]),
                ulr=np.ascontiguousarray(
                    np.asarray(inputs["L0_attn_d_lr_u"]).reshape(KD, P, 4)),
                vlr=np.ascontiguousarray(np.asarray(inputs["L0_attn_d_lr_v"])))
    in_maps = []
    for i in range(NCORES):
        m = dict(base)
        m["xT"] = np.ascontiguousarray(xTfull[:, i * tcore:(i + 1) * tcore])
        mask = np.zeros((P, KD, NCORES), dtype=np.float32)
        mask[:, :, :i] = 1.0
        m["maskin"] = mask.reshape(P, KD * NCORES)
        in_maps.append(m)
    return in_maps


def kernel(**inputs):
    u = np.asarray(inputs["L0_attn_u"])
    T = u.shape[1]
    tcore = T // NCORES
    tt = min(512, tcore)
    in_maps = _host_layouts(inputs, tcore)
    nc = build_nc(tcore, tt, T)
    res = run_bass_kernel_spmd(nc, in_maps, core_ids=list(range(NCORES)))
    outT = np.concatenate([r["outT"] for r in res.results], axis=1)
    return np.ascontiguousarray(outT.T).reshape(1, T, D).astype(np.float32)
